# version 54
# baseline (speedup 1.0000x reference)
import numpy as np
import concourse.bass as bass
import concourse.mybir as mybir
from concourse.bass_utils import run_bass_kernel_spmd
from contextlib import ExitStack

F32 = mybir.dt.float32
BF16 = mybir.dt.bfloat16
AX = mybir.AxisListType.X
AF = mybir.ActivationFunctionType
OP = mybir.AluOpType

D = 1024
R = 64
DC = 8
ENGINES = ("sync", "gpsimd", "vector", "scalar", "tensor")


class Sched:

    def __init__(self):
        self.ops = {e: [] for e in ENGINES}
        self.count = {}
        self.wmax = {e: {} for e in ENGINES}
        self.tw = {}
        self.tr = {}
        self.sems = set(ENGINES)

    def emit(self, eng, fn, reads=(), writes=(), sem=None, dma=False):
        sem = sem or eng
        self.sems.add(sem)
        deps = {}

        def add(psem, pv):
            if psem == "tensor" and eng == "tensor":
                return
            if pv > deps.get(psem, 0):
                deps[psem] = pv

        for t in reads:
            w = self.tw.get(t)
            if w is not None:
                add(*w)
        for t in writes:
            w = self.tw.get(t)
            if w is not None:
                add(*w)
            for r in self.tr.get(t, ()):
                add(*r)
        step = 16 if dma else 1
        v = self.count.get(sem, 0) + step
        self.count[sem] = v
        waits = []
        wm = self.wmax[eng]
        for psem, pv in sorted(deps.items()):
            if pv > wm.get(psem, 0):
                waits.append((psem, pv))
                wm[psem] = pv
        self.ops[eng].append((waits, fn, sem, step))
        for t in writes:
            self.tw[t] = (sem, v)
            self.tr[t] = []
        for t in reads:
            self.tr.setdefault(t, []).append((sem, v))
        return v


def build_nc(nt=32, nq=16):
    T = 128 * nt
    TQ = 128 * nq
    nc = bass.Bass("TRN2", target_bir_lowering=False)

    xb = nc.dram_tensor("xb", [T, D], F32, kind="ExternalInput").ap()
    qw = nc.dram_tensor("qw", [R, D], F32, kind="ExternalInput").ap()
    kw = nc.dram_tensor("kw", [R, D], F32, kind="ExternalInput").ap()
    vo = nc.dram_tensor("vo", [D, D], F32, kind="ExternalInput").ap()
    ww = nc.dram_tensor("ww", [D, D], F32, kind="ExternalInput").ap()
    y = nc.dram_tensor("y", [TQ, D], F32, kind="ExternalOutput").ap()

    es = ExitStack()
    with es:
        def sb(name, shape, dt=F32):
            return es.enter_context(nc.sbuf_tensor(name, shape, dt))

        def ps(name, shape, dt=F32):
            return es.enter_context(nc.psum_tensor(name, shape, dt))

        V = sb("V", [128, nt * 1024], BF16)
        VOc = sb("VOc", [128, DC * 1024], BF16)
        wqT = sb("wqT", [128, DC * 1024], BF16)
        Pbf = sb("Pbf", [128, nt * 128], BF16)
        PT = sb("PT", [128, nt * 128], BF16)
        xkT = sb("xkT", [64, nt * 128], BF16)
        xqT = sb("xqT", [64, nq * 128], BF16)
        xstg = [sb(f"xstg{p}", [128, D]) for p in range(2)]
        xres = [sb(f"xres{p}", [128, D]) for p in range(2)]
        wstg = [sb(f"wstg{p}", [128, D]) for p in range(2)]
        hb = [sb(f"hb{p}", [128, D]) for p in range(2)]
        hn = [sb(f"hn{p}", [128, D]) for p in range(2)]
        yp = [sb(f"yp{p}", [128, D]) for p in range(2)]
        sbuf_s = sb("sbuf_s", [128, D])
        cbuf = sb("cbuf", [128, D])
        aq = [sb(f"aq{p}", [128, D], BF16) for p in range(2)]
        aqT = sb("aqT", [128, D], BF16)
        outb = [sb(f"outb{p}", [128, D]) for p in range(2)]
        xnT = [sb(f"xnT{p}", [128, D], BF16) for p in range(2)]
        QKT = sb("QKT", [128, 1024], BF16)
        ident = sb("ident", [128, 128], BF16)
        ones = sb("ones", [128, 1])
        onesr = sb("onesr", [1, 128])
        asum8 = sb("asum8", [128, 8])
        easum = sb("easum", [128, 16])
        wsA = sb("wsA", [1, 4])
        wsumv = sb("wsumv", [128, 1])
        bcast2 = sb("bcast2", [128, 2])
        cb7 = sb("cb7", [128, 7])
        st1 = [sb(f"st1{p}", [128, 24]) for p in range(2)]
        st2 = [sb(f"st2{p}", [128, 32]) for p in range(2)]

        ps_s01 = ps("ps_s01", [128, 1024])
        ps_a01 = ps("ps_a01", [128, 1024])
        ps_a = [ps_a01[:, 0:512], ps_a01[:, 512:1024]]
        ps_t = [ps("ps_t0", [128, 1024], BF16), ps("ps_t1", [128, 1024], BF16)]
        ps_kq = ps("ps_kq", [128, 1024])

        S = Sched()

        S.emit("gpsimd", lambda e: e.memset(ident[:], 0.0), writes=["ident"])
        S.emit("gpsimd", lambda e: e.affine_select(
            out=ident[:], in_=ident[:], compare_op=OP.not_equal, fill=1.0,
            base=0, pattern=[[-1, 128]], channel_multiplier=1),
            reads=["ident"], writes=["ident"])
        S.emit("vector", lambda e: e.memset(ones[:], 1.0), writes=["ones"])
        S.emit("vector", lambda e: e.memset(easum[:], 0.0),
               writes=[f"easum{p}_{c}" for p in range(2) for c in range(5)])
        for k in range(7):
            S.emit("vector", lambda e, k=k: e.memset(cb7[:, k:k + 1], -float(1 << (6 - k))),
                   writes=[f"cb7_{k}"])
        S.emit("vector", lambda e: e.memset(onesr[:], 1.0), writes=["onesr"])

        S.emit("sync", lambda e: e.dma_start(out=xres[0][0:64, :], in_=qw[:]),
               writes=["xres0"], sem="dq", dma=True)
        S.emit("sync", lambda e: e.dma_start(out=xres[1][0:64, :], in_=kw[:]),
               writes=["xres1"], sem="dk", dma=True)
        S.emit("vector", lambda e: e.tensor_scalar_mul(
            out=aq[0][0:64, :], in0=xres[0][0:64, :], scalar1=1.0 / 32.0),
            reads=["xres0"], writes=["aq0"])
        S.emit("vector", lambda e: e.tensor_copy(out=aq[1][0:64, :], in_=xres[1][0:64, :]),
               reads=["xres1"], writes=["aq1"])
        for j in range(DC):
            def tpqk(e, j=j):
                return e.transpose(ps_t[0][:, 64 * j:64 * j + 64],
                                   aq[0][0:64, 128 * j:128 * j + 128], ident[0:64, 0:64])
            S.emit("tensor", tpqk, reads=["aq0", "ident"], writes=[f"ps_t0_s{j}"])
        for j in range(DC):
            def tpqk2(e, j=j):
                return e.transpose(ps_t[1][:, 64 * j:64 * j + 64],
                                   aq[1][0:64, 128 * j:128 * j + 128], ident[0:64, 0:64])
            S.emit("tensor", tpqk2, reads=["aq1", "ident"], writes=[f"ps_t1_s{j}"])
        S.emit("vector", lambda e: e.tensor_copy(out=QKT[:, 0:512], in_=ps_t[0][:, 0:512]),
               reads=[f"ps_t0_s{j}" for j in range(DC)], writes=["QKT"])
        S.emit("vector", lambda e: e.tensor_copy(out=QKT[:, 512:1024], in_=ps_t[1][:, 0:512]),
               reads=[f"ps_t1_s{j}" for j in range(DC)], writes=["QKT"])

        def emit_vo_chunk(j):
            b = j % 4
            stg = [wstg[0], wstg[1], hb[0], hb[1]][b]
            wtok = [["wstg0"], ["wstg1"], ["hb0_0", "hb0_1"], ["hb1_0", "hb1_1"]][b]
            def ldvo(e, j=j, stg=stg):
                return e.dma_start(out=stg[:], in_=vo[128 * j:128 * (j + 1), :])
            S.emit("sync", ldvo, writes=wtok, sem=["dw0", "dw1", "dv0", "dv1"][b], dma=True)
            if j % 2 == 0:
                def cvo(e, j=j, stg=stg):
                    return e.copy(out=VOc[:, 1024 * j:1024 * (j + 1)], in_=stg[:])
                S.emit("scalar", cvo, reads=wtok, writes=[f"VOc{j}"])
            else:
                def cvo(e, j=j, stg=stg):
                    return e.tensor_copy(out=VOc[:, 1024 * j:1024 * (j + 1)], in_=stg[:])
                S.emit("vector", cvo, reads=wtok, writes=[f"VOc{j}"])

        def emit_vo_weights():
            for j in range(DC):
                emit_vo_chunk(j)

        def wabs_chunk(j):
            def ldw(e, j=j):
                return e.dma_start(out=wstg[j % 2][:], in_=ww[128 * j:128 * (j + 1), :])
            S.emit("sync", ldw, writes=[f"wstg{j % 2}"], sem=f"dw{j % 2}", dma=True)
            def absw(e, j=j):
                return e.activation(out=cbuf[:], in_=wstg[j % 2][:], func=AF.Abs,
                                    accum_out=asum8[:, j:j + 1])
            S.emit("scalar", absw, reads=[f"wstg{j % 2}"], writes=["cbuf", f"asum8_{j}"])

        def wscale_finish():
            S.emit("vector", lambda e: e.reduce_sum(out=wsumv[:], in_=asum8[:], axis=AX),
                   reads=[f"asum8_{j}" for j in range(DC)], writes=["wsum"])
            S.emit("tensor", lambda e: e.matmul(ps_kq[0:1, 0:1], wsumv[:], ones[:],
                                                start=True, stop=True),
                   reads=["wsum", "ones"], writes=["ps_kq_0"])
            S.emit("scalar", lambda e: e.activation(out=wsA[0:1, 0:1], in_=ps_kq[0:1, 0:1],
                                                    func=AF.Copy, bias=1e-8,
                                                    scale=1.0 / (D * D)),
                   reads=["ps_kq_0"], writes=["ws"])
            S.emit("vector", lambda e: e.reciprocal(out=wsA[0:1, 1:2], in_=wsA[0:1, 0:1]),
                   reads=["ws"], writes=["rws"])
            S.emit("scalar", lambda e: e.mul(out=wsA[0:1, 2:3], in_=wsA[0:1, 0:1],
                                             mul=1.0 / 127.0),
                   reads=["ws"], writes=["ws127"])
            S.emit("tensor", lambda e: e.matmul(ps_s01[:, 0:2], onesr[0:1, 0:128],
                                                wsA[0:1, 1:3], start=True, stop=True),
                   reads=["rws", "ws127", "onesr"], writes=["ps_s01_0"])
            S.emit("vector", lambda e: e.tensor_copy(out=bcast2[:], in_=ps_s01[:, 0:2]),
                   reads=["ps_s01_0"], writes=["bcast2"])

        def wquant_v(co):
            def ldw2(e, co=co):
                return e.dma_start(out=wstg[co % 2][:], in_=ww[128 * co:128 * (co + 1), :])
            S.emit("sync", ldw2, writes=[f"wstg{co % 2}"], sem=f"dw{co % 2}", dma=True)
            def uop(e, co=co):
                return e.tensor_scalar_mul(out=cbuf[:], in0=wstg[co % 2][:],
                                           scalar1=bcast2[:, 0:1])
            S.emit("vector", uop, reads=[f"wstg{co % 2}", "bcast2"], writes=["cbuf"])
            S.emit("vector", lambda e: e.tensor_scalar(
                out=sbuf_s[:], in0=cbuf[:], scalar1=0.5, scalar2=None, op0=OP.is_ge),
                reads=["cbuf"], writes=["s_a", "s_b"])
            S.emit("vector", lambda e: e.tensor_scalar(
                out=hb[0][:], in0=cbuf[:], scalar1=-0.5, scalar2=None, op0=OP.is_le),
                reads=["cbuf"], writes=["hb0_0", "hb0_1"])
            S.emit("vector", lambda e: e.tensor_tensor(
                out=aqT[:], in0=sbuf_s[:], in1=hb[0][:], op=OP.subtract),
                reads=["s_a", "s_b", "hb0_0", "hb0_1"], writes=["aqT"])

        def wquant_tp(co):
            parw = co % 2
            for ci in range(DC):
                def tpw(e, co=co, ci=ci, parw=parw):
                    return e.transpose(ps_t[parw][:, 128 * ci:128 * ci + 128],
                                       aqT[:, 128 * ci:128 * ci + 128], ident[:])
                S.emit("tensor", tpw, reads=["aqT", "ident"],
                       writes=[f"ps_t{parw}_s{ci}"])
            def evw(e, co=co, parw=parw):
                src = ps_t[parw][:].rearrange("p (ci f) -> p ci f", f=128)
                dst = wqT[:].rearrange("p (ci f) -> p ci f", f=1024)[:, :, 128 * co:128 * co + 128]
                return e.tensor_copy(out=dst, in_=src)
            S.emit("vector", evw, reads=[f"ps_t{parw}_s{ci}" for ci in range(DC)],
                   writes=["wqT"])

        def wquant_chunk(co):
            wquant_v(co)
            wquant_tp(co)

        def p1_load(i):
            par = i % 2
            def ldx(e, i=i, par=par):
                return e.dma_start(out=xstg[par][:], in_=xb[128 * i:128 * (i + 1), :])
            S.emit("sync", ldx, writes=[f"xstg{par}"], sem=f"dx{par}", dma=True)

        def p1_stats(i):
            par = i % 2
            stp = st1[par]
            stok = [f"st1{par}_{k}" for k in range(18)]
            xt = f"xstg{par}"
            S.emit("vector", lambda e, par=par, stp=stp: e.bn_stats(
                out=stp[:, 0:6], in_=xstg[par][:, 0:512]),
                reads=[xt], writes=[stok[0]])
            S.emit("vector", lambda e, par=par, stp=stp: e.bn_stats(
                out=stp[:, 6:12], in_=xstg[par][:, 512:1024]),
                reads=[xt], writes=[stok[1]])
            S.emit("vector", lambda e, stp=stp: e.bn_aggr(
                out=stp[:, 12:14], in_=stp[:, 0:12]),
                reads=[stok[0], stok[1]], writes=[stok[12]])
            S.emit("vector", lambda e, stp=stp: e.tensor_scalar_mul(
                out=stp[:, 14:15], in0=stp[:, 12:13], scalar1=-1.0),
                reads=[stok[12]], writes=[stok[14]])
            S.emit("vector", lambda e, stp=stp: e.tensor_scalar_add(
                out=stp[:, 15:16], in0=stp[:, 13:14], scalar1=1e-5),
                reads=[stok[12]], writes=[stok[15]])
            S.emit("scalar", lambda e, stp=stp: e.activation(out=stp[:, 16:17], in_=stp[:, 15:16], func=AF.Ln),
                   reads=[stok[15]], writes=[stok[16]])
            S.emit("scalar", lambda e, stp=stp: e.activation(out=stp[:, 17:18], in_=stp[:, 16:17], func=AF.Exp, scale=-0.5),
                   reads=[stok[16]], writes=[stok[17]])

        def p1_xn(i):
            par = i % 2
            stp = st1[par]
            stok = [f"st1{par}_{k}" for k in range(18)]
            xt = f"xstg{par}"
            def xnop2(e, par=par, stp=stp):
                return e.tensor_scalar(out=aq[par][:], in0=xstg[par][:],
                                       scalar1=stp[:, 14:15], scalar2=stp[:, 17:18],
                                       op0=OP.add, op1=OP.mult)
            S.emit("vector", xnop2, reads=[xt, stok[14], stok[17]],
                   writes=[f"aq{par}", f"aq{par}_v", f"aq{par}_g"])
            for j in range(DC):
                def tpx(e, j=j, par=par):
                    return e.transpose(ps_t[par][:, 128 * j:128 * j + 128],
                                       aq[par][:, 128 * j:128 * j + 128], ident[:])
                S.emit("tensor", tpx, reads=[f"aq{par}", f"aq{par}_v", f"aq{par}_g", "ident"],
                       writes=[f"ps_t{par}_s{j}"])
            def evx(e, par=par):
                return e.copy(out=xnT[par][:], in_=ps_t[par][:])
            S.emit("scalar", evx, reads=[f"ps_t{par}_s{j}" for j in range(DC)],
                   writes=[f"xnT{par}"])

        def p1_mm(i):
            par = i % 2
            for half in range(2):
                def vmm(e, par=par, half=half):
                    last = None
                    for j in range(4 * half, 4 * half + 4):
                        for eh in range(2):
                            last = e.matmul(ps_a[eh],
                                            xnT[par][:, 128 * j:128 * j + 128],
                                            VOc[:, 1024 * j + 512 * eh:1024 * j + 512 * eh + 512],
                                            start=(j == 0), stop=(j == DC - 1))
                    return last
                S.emit("tensor", vmm,
                       reads=[f"xnT{par}"] + [f"VOc{j}" for j in range(4 * half, 4 * half + 4)],
                       writes=["ps_a0", "ps_a1"])
            for eh in range(2):
                def evv(e, i=i, eh=eh):
                    return e.copy(out=V[:, 1024 * i + 512 * eh:1024 * i + 512 * eh + 512],
                                  in_=ps_a[eh])
                S.emit("scalar", evv, reads=[f"ps_a{eh}"], writes=[f"V{i}e{eh}"])
            def kmm(e, par=par):
                last = None
                for j in range(DC):
                    last = e.matmul(ps_kq[0:64, 0:128],
                                    QKT[:, 512 + 64 * j:512 + 64 * j + 64],
                                    xnT[par][:, 128 * j:128 * j + 128],
                                    start=(j == 0), stop=(j == DC - 1))
                return last
            S.emit("tensor", kmm, reads=[f"xnT{par}", "QKT"], writes=["ps_kq_0"])
            def evk(e, i=i):
                return e.copy(out=xkT[0:64, 128 * i:128 * i + 128],
                              in_=ps_kq[0:64, 0:128])
            S.emit("scalar", evk, reads=["ps_kq_0"], writes=[f"xk{i}"])
            if i < nq:
                def qmm(e, par=par):
                    last = None
                    for j in range(DC):
                        last = e.matmul(ps_kq[0:64, 512:640],
                                        QKT[:, 64 * j:64 * j + 64],
                                        xnT[par][:, 128 * j:128 * j + 128],
                                        start=(j == 0), stop=(j == DC - 1))
                    return last
                S.emit("tensor", qmm, reads=[f"xnT{par}", "QKT"], writes=["ps_kq_1"])
                def evq(e, i=i):
                    return e.copy(out=xqT[0:64, 128 * i:128 * i + 128],
                                  in_=ps_kq[0:64, 512:640])
                S.emit("scalar", evq, reads=["ps_kq_1"], writes=[f"xq{i}"])

        p1_load(0)
        p1_stats(0)
        if nt > 1:
            p1_load(1)
        interleave = nt >= 28
        if not interleave:
            emit_vo_weights()
        for i in range(nt):
            p1_xn(i)
            if interleave and i >= 17 and i % 2 == 1 and (i - 17) // 2 < DC:
                wquant_v((i - 17) // 2)
            if i + 2 < nt:
                p1_load(i + 2)
            if i + 1 < nt:
                p1_stats(i + 1)
            if interleave and i == 0:
                for j in range(DC):
                    emit_vo_chunk(j)
            if i >= 1:
                p1_mm(i - 1)
            if interleave:
                if 8 <= i < 8 + DC:
                    wabs_chunk(i - 8)
                if i == 8 + DC:
                    wscale_finish()
                if i >= 17 and i % 2 == 1 and (i - 17) // 2 < DC:
                    wquant_tp((i - 17) // 2)
        p1_mm(nt - 1)
        if not interleave:
            for j in range(DC):
                wabs_chunk(j)
            wscale_finish()
            for co in range(DC):
                wquant_chunk(co)

        NKC = nt // 4
        ng = nt // 8

        def stA1a(t):
            par = t % 2
            def ldr(e, t=t, par=par):
                return e.dma_start(out=xres[par][:], in_=xb[128 * t:128 * (t + 1), :])
            S.emit("sync", ldr, writes=[f"xres{par}"], sem=f"dr{par}", dma=True)
            def smm0(e, t=t):
                return e.matmul(ps_s01[:, 0:512], xqT[0:64, 128 * t:128 * t + 128],
                                xkT[0:64, 0:512], start=True, stop=True)
            S.emit("tensor", smm0,
                   reads=[f"xq{t}"] + [f"xk{l}" for l in range(4)],
                   writes=["ps_s01_0"])

        def stA1ne(t):
            par = t % 2
            stp = st2[par]
            stok = [f"st2{par}_{k}" for k in range(28)]
            S.emit("vector", lambda e, stp=stp: e.reduce_max(
                out=stp[:, 0:1], in_=ps_s01[:, 0:128], axis=AX, negate=True),
                reads=["ps_s01_0"], writes=[stok[0]])
            for hh in range(2):
                def expp(e, par=par, stp=stp, hh=hh):
                    return e.activation(
                        out=Pbf[:, 512 * hh:512 * hh + 512],
                        in_=ps_s01[:, 512 * hh:512 * hh + 512], func=AF.Exp,
                        bias=stp[:, 0:1],
                        accum_out=easum[:, 8 * par + 4 * hh:8 * par + 4 * hh + 1])
                S.emit("scalar", expp,
                       reads=[f"ps_s01_{hh}", stok[0]],
                       writes=[f"P{hh}", f"easum{par}_{4 * hh}"])

        def _a1b_chunk(t, c):
            par = t % 2
            stp = st2[par]
            stok = [f"st2{par}_{k}" for k in range(28)]
            def sbank(c):
                pa = c in (4, 5)
                tens = ps_a01 if pa else ps_s01
                off = 512 * (c % 2)
                return tens[:, off:off + 512], ("ps_a" if pa else "ps_s01_") + str(c % 2)
            if c > 0:
                def smm(e, t=t, c=c):
                    ap, _ = sbank(c)
                    return e.matmul(ap, xqT[0:64, 128 * t:128 * t + 128],
                                    xkT[0:64, 512 * c:512 * c + 512],
                                    start=True, stop=True)
                S.emit("tensor", smm,
                       reads=[f"xq{t}"] + [f"xk{4 * c + l}" for l in range(4)],
                       writes=[sbank(c)[1]])
            if c % 2 == 1 and c > 1:
                cp = c // 2
                pa = c in (4, 5)
                tens = ps_a01 if pa else ps_s01
                ttok = "ps_a" if pa else "ps_s01"
                def expp(e, cp=cp, par=par, stp=stp, tens=tens):
                    return e.activation(out=Pbf[:, 1024 * cp:1024 * cp + 1024],
                                        in_=tens[:], func=AF.Exp,
                                        bias=stp[:, 0:1],
                                        accum_out=easum[:, 8 * par + cp:8 * par + cp + 1])
                rd0 = ttok + ("0" if ttok == "ps_a" else "_0")
                rd1 = ttok + ("1" if ttok == "ps_a" else "_1")
                S.emit("scalar", expp,
                       reads=[rd0, rd1, stok[0]],
                       writes=[f"P{2 * cp}", f"P{2 * cp + 1}", f"easum{par}_{cp}"])

        def stA1b0(t):
            for c in range(min(2, NKC)):
                _a1b_chunk(t, c)

        def stA1b1(t):
            _tp_group(t, 0)
            for c in range(2, NKC):
                _a1b_chunk(t, c)

        def _tp_group(t, g):
            gp = g % 2
            for sl in range(8):
                kb = 8 * g + sl
                def tpp(e, kb=kb, sl=sl, gp=gp):
                    return e.transpose(ps_t[gp][:, 128 * sl:128 * sl + 128],
                                       Pbf[:, 128 * kb:128 * kb + 128], ident[:])
                S.emit("tensor", tpp, reads=[f"P{kb // 4}", "ident"],
                       writes=[f"ps_t{gp}_s{sl}"])
            def evp(e, g=g, gp=gp):
                return e.copy(out=PT[:, 1024 * g:1024 * (g + 1)], in_=ps_t[gp][:])
            S.emit("scalar", evp, reads=[f"ps_t{gp}_s{sl}" for sl in range(8)],
                   writes=[f"PT{g}"])

        def _pv_group(t, g, eh):
            def pvmm(e, g=g, eh=eh):
                last = None
                for sl in range(8):
                    kb = 8 * g + sl
                    last = e.matmul(
                        ps_a[eh], PT[:, 128 * kb:128 * kb + 128],
                        V[:, 1024 * kb + 512 * eh:1024 * kb + 512 * eh + 512],
                        start=(kb == 0), stop=(kb == nt - 1))
                return last
            S.emit("tensor", pvmm,
                   reads=[f"PT{g}"] + [f"V{8 * g + sl}e{eh}" for sl in range(8)],
                   writes=[f"ps_a{eh}"])

        def stB_tp(t):
            for g in range(1, ng):
                _tp_group(t, g)
                _pv_group(t, g - 1, 0)
            _pv_group(t, ng - 1, 0)

        def stB_pv(t, eh):
            if eh == 0:
                return
            for g in range(ng):
                _pv_group(t, g, eh)

        def stC1r(t):
            par = t % 2
            stp = st2[par]
            stok = [f"st2{par}_{k}" for k in range(28)]
            S.emit("vector", lambda e, stp=stp, par=par: e.reduce_sum(
                out=stp[:, 1:2], in_=easum[:, 8 * par:8 * par + 5], axis=AX),
                   reads=[f"easum{par}_{c}" for c in range(NKC // 2)]
                         + [f"easum{par}_4"], writes=[stok[1]])
            S.emit("vector", lambda e, stp=stp: e.reciprocal(out=stp[:, 2:3], in_=stp[:, 1:2]),
                   reads=[stok[1]], writes=[stok[2]])

        def stC1h(t, eh):
            par = t % 2
            stp = st2[par]
            stok = [f"st2{par}_{k}" for k in range(28)]
            def hop(e, eh=eh, par=par, stp=stp):
                return e.scalar_tensor_tensor(
                    out=hb[par][:, 512 * eh:512 * eh + 512], in0=ps_a[eh],
                    scalar=stp[:, 2:3], in1=xres[par][:, 512 * eh:512 * eh + 512],
                    op0=OP.mult, op1=OP.add,
                    accum_out=stp[:, 3 + eh:4 + eh])
            S.emit("vector", hop, reads=[f"ps_a{eh}", stok[2], f"xres{par}"],
                   writes=[f"hb{par}_{eh}", stok[3 + eh]])
            def sqo(e, eh=eh, par=par, stp=stp):
                return e.activation(out=xstg[eh][:, 0:512],
                                    in_=hb[par][:, 512 * eh:512 * eh + 512],
                                    func=AF.Square,
                                    accum_out=stp[:, 5 + eh:6 + eh])
            S.emit("scalar", sqo, reads=[f"hb{par}_{eh}"],
                   writes=[stok[5 + eh], f"xstg{eh}"])

        def stC1a2(t):
            par = t % 2
            stp = st2[par]
            stok = [f"st2{par}_{k}" for k in range(28)]
            S.emit("vector", lambda e, stp=stp: e.tensor_tensor(
                out=stp[:, 7:8], in0=stp[:, 3:4], in1=stp[:, 4:5], op=OP.add),
                reads=[stok[3], stok[4]], writes=[stok[7]])
            S.emit("vector", lambda e, stp=stp: e.tensor_scalar_mul(
                out=stp[:, 17:18], in0=stp[:, 7:8], scalar1=-1.0 / D),
                reads=[stok[7]], writes=[stok[17]])

        def stC1a2b(t):
            par = t % 2
            stp = st2[par]
            stok = [f"st2{par}_{k}" for k in range(28)]
            S.emit("vector", lambda e, stp=stp: e.tensor_tensor(
                out=stp[:, 8:9], in0=stp[:, 5:6], in1=stp[:, 6:7], op=OP.add),
                reads=[stok[5], stok[6]], writes=[stok[8]])
            S.emit("vector", lambda e, stp=stp: e.tensor_scalar(
                out=stp[:, 16:17], in0=stp[:, 8:9], scalar1=1.0 / D,
                scalar2=1e-5, op0=OP.mult, op1=OP.add),
                reads=[stok[8]], writes=[stok[16]])
            S.emit("vector", lambda e, stp=stp: e.tensor_tensor(
                out=stp[:, 25:26], in0=stp[:, 17:18], in1=stp[:, 17:18], op=OP.mult),
                reads=[stok[17]], writes=[stok[25]])
            S.emit("vector", lambda e, stp=stp: e.scalar_tensor_tensor(
                out=stp[:, 18:19], in0=stp[:, 25:26], scalar=-1.0,
                in1=stp[:, 16:17], op0=OP.mult, op1=OP.add),
                reads=[stok[25], stok[16]], writes=[stok[18]])
            S.emit("scalar", lambda e, stp=stp: e.activation(out=stp[:, 19:20], in_=stp[:, 18:19], func=AF.Ln),
                   reads=[stok[18]], writes=[stok[19]])
            S.emit("scalar", lambda e, stp=stp: e.activation(out=stp[:, 20:21], in_=stp[:, 19:20], func=AF.Exp, scale=-0.5),
                   reads=[stok[19]], writes=[stok[20]])

        def stC1b_a(t):
            par = t % 2
            stp = st2[par]
            stok = [f"st2{par}_{k}" for k in range(28)]
            def uop2(e, par=par, stp=stp):
                return e.tensor_scalar_add(out=hn[par][:], in0=hb[par][:],
                                           scalar1=stp[:, 17:18])
            S.emit("vector", uop2, reads=[f"hb{par}_0", f"hb{par}_1", stok[17]],
                   writes=[f"hn{par}"])
            S.emit("vector", lambda e, par=par, stp=stp: e.reduce_max(
                out=stp[:, 21:22], in_=hn[par][:], axis=AX, apply_absolute_value=True),
                reads=[f"hn{par}"], writes=[stok[21]])

        def stC1b_b(t):
            par = t % 2
            stp = st2[par]
            stok = [f"st2{par}_{k}" for k in range(28)]
            S.emit("vector", lambda e, stp=stp: e.tensor_scalar(
                out=stp[:, 22:23], in0=stp[:, 21:22], scalar1=stp[:, 20:21],
                scalar2=1e-8, op0=OP.mult, op1=OP.add),
                reads=[stok[21], stok[20]], writes=[stok[22]])
            S.emit("vector", lambda e, stp=stp: e.reciprocal(out=stp[:, 26:27], in_=stp[:, 22:23]),
                   reads=[stok[22]], writes=[stok[26]])
            S.emit("vector", lambda e, stp=stp: e.tensor_scalar(
                out=stp[:, 23:24], in0=stp[:, 26:27], scalar1=stp[:, 20:21],
                scalar2=127.0, op0=OP.mult, op1=OP.mult),
                reads=[stok[26], stok[20]], writes=[stok[23]])
            def ypop(e, par=par, stp=stp):
                return e.tensor_scalar(out=yp[par][:], in0=hn[par][:],
                                       scalar1=stp[:, 23:24], scalar2=0.5,
                                       op0=OP.mult, op1=OP.add)
            S.emit("vector", ypop, reads=[f"hn{par}", stok[23]], writes=[f"yp{par}"])
            S.emit("vector", lambda e, stp=stp: e.tensor_tensor(
                out=stp[:, 24:25], in0=stp[:, 22:23], in1=bcast2[:, 1:2], op=OP.mult),
                reads=[stok[22], "bcast2"], writes=[stok[24]])

        def stC2(t):
            par = t % 2
            S.emit("vector", lambda e, par=par: e.tensor_scalar(
                out=cbuf[:], in0=yp[par][:], scalar1=0.0, scalar2=128.0,
                op0=OP.is_lt, op1=OP.mult),
                reads=[f"yp{par}"], writes=["cbuf"])
            S.emit("vector", lambda e, par=par: e.tensor_tensor(
                out=sbuf_s[:], in0=yp[par][:], in1=cbuf[:], op=OP.add),
                reads=[f"yp{par}", "cbuf"], writes=["s_a", "s_b"])
            for k in range(6, -1, -1):
                tw = float(1 << k)
                S.emit("vector", lambda e, tw=tw: e.tensor_scalar(
                    out=cbuf[:], in0=sbuf_s[:], scalar1=tw, scalar2=tw,
                    op0=OP.is_ge, op1=OP.mult),
                    reads=["s_a", "s_b"], writes=["cbuf"])
                S.emit("vector", lambda e: e.tensor_tensor(
                    out=sbuf_s[:], in0=sbuf_s[:], in1=cbuf[:], op=OP.subtract),
                    reads=["s_a", "s_b", "cbuf"], writes=["s_a", "s_b"])
            S.emit("vector", lambda e, par=par: e.scalar_tensor_tensor(
                out=aq[par][:], in0=sbuf_s[:], scalar=-1.0, in1=yp[par][:],
                op0=OP.mult, op1=OP.add),
                reads=["s_a", "s_b", f"yp{par}"],
                writes=[f"aq{par}", f"aq{par}_v", f"aq{par}_g"])

        def stD_tp(t):
            par = t % 2
            for j in range(DC):
                h, sl = divmod(j, 4)
                def tpa(e, j=j, h=h, sl=sl, par=par):
                    return e.transpose(ps_t[h][:, 128 * sl:128 * sl + 128],
                                       aq[par][:, 128 * j:128 * j + 128], ident[:])
                S.emit("tensor", tpa, reads=[f"aq{par}_v", f"aq{par}_g", "ident"],
                       writes=[f"ps_t{h}_s{sl}"])
            for h in range(2):
                def eva(e, h=h):
                    return e.copy(out=aqT[:, 512 * h:512 * h + 512],
                                  in_=ps_t[h][:, 0:512])
                S.emit("scalar", eva,
                       reads=[f"ps_t{h}_s{sl}" for sl in range(4)],
                       writes=[f"aqT_h{h}"])

        def stD_ffn(t):
            def fmm(e, j0, j1):
                last = None
                for j in range(j0, j1):
                    for oh in range(2):
                        last = e.matmul(
                            ps_kq[:, 512 * oh:512 * oh + 512],
                            aqT[:, 128 * j:128 * j + 128],
                            wqT[:, 1024 * j + 512 * oh:1024 * j + 512 * oh + 512],
                            start=(j == 0), stop=(j == DC - 1))
                return last
            S.emit("tensor", lambda e: fmm(e, 0, DC // 2),
                   reads=["aqT_h0", "wqT"], writes=["ps_kq_0", "ps_kq_1"])
            S.emit("tensor", lambda e: fmm(e, DC // 2, DC),
                   reads=["aqT_h1", "wqT"], writes=["ps_kq_0", "ps_kq_1"])

        def stD_out(t):
            par = t % 2
            stp = st2[par]
            stok = [f"st2{par}_{k}" for k in range(28)]
            for eh in range(2):
                def oop(e, eh=eh, par=par, stp=stp):
                    return e.scalar_tensor_tensor(
                        out=outb[par][:, 512 * eh:512 * eh + 512],
                        in0=ps_kq[:, 512 * eh:512 * eh + 512],
                        scalar=stp[:, 24:25],
                        in1=hb[par][:, 512 * eh:512 * eh + 512],
                        op0=OP.mult, op1=OP.add)
                S.emit("vector", oop, reads=[f"ps_kq_{eh}", stok[24],
                                             f"hb{par}_{eh}"],
                       writes=[f"outb{par}_e{eh}"])
            def stx(e, t=t, par=par):
                return e.dma_start(out=y[128 * t:128 * (t + 1), :], in_=outb[par][:])
            S.emit("gpsimd", stx, reads=[f"outb{par}_e0", f"outb{par}_e1"],
                   sem=f"do{par}", dma=True)

        stA1a(0)
        stA1b0(0)
        stA1ne(0)
        stA1b1(0)
        for t in range(nq):
            stB_tp(t)
            stC1r(t)
            stC1h(t, 0)
            stB_pv(t, 1)
            if t + 1 < nq:
                stA1a(t + 1)
                stA1b0(t + 1)
            stC1h(t, 1)
            if t > 0:
                stD_tp(t - 1)
            stC1a2(t)
            if t > 0:
                stD_ffn(t - 1)
            if t + 1 < nq:
                stA1ne(t + 1)
            stC1b_a(t)
            stC1a2b(t)
            stC1b_b(t)
            if t > 0:
                stD_out(t - 1)
            if t + 1 < nq:
                stA1b1(t + 1)
            stC2(t)
        stD_tp(nq - 1)
        stD_ffn(nq - 1)
        stD_out(nq - 1)

        sems = {name: es.enter_context(nc.semaphore(f"sem_{name}"))
                for name in sorted(S.sems)}
        block = es.enter_context(nc.Block())
        final_waits = [(sn, S.count[sn]) for sn in sorted(S.count)
                       if sn.startswith("do")]

        def section(name):
            def body(eng):
                for waits, fn, sem, step in S.ops[name]:
                    for psem, pv in waits:
                        eng.wait_ge(sems[psem], pv)
                    inst = fn(eng)
                    inst.then_inc(sems[sem], step)
                if name == "gpsimd":
                    for sn, val in final_waits:
                        eng.wait_ge(sems[sn], val)
            return body

        block.sync(section("sync"))
        block.gpsimd(section("gpsimd"))
        block.vector(section("vector"))
        block.scalar(section("scalar"))
        block.tensor(section("tensor"))
    return nc


_NC_CACHE = {}


def get_nc(nt=32, nq=16):
    key = (nt, nq)
    if key not in _NC_CACHE:
        _NC_CACHE[key] = build_nc(nt, nq)
    return _NC_CACHE[key]


def kernel(L0_attn_x, L0_attn_Q, L0_attn_K, L0_attn_VO, L0_ffn_weight):
    x = np.ascontiguousarray(np.asarray(L0_attn_x, dtype=np.float32))
    q = np.ascontiguousarray(np.asarray(L0_attn_Q, dtype=np.float32))
    k = np.ascontiguousarray(np.asarray(L0_attn_K, dtype=np.float32))
    vo = np.ascontiguousarray(np.asarray(L0_attn_VO, dtype=np.float32))
    w = np.ascontiguousarray(np.asarray(L0_ffn_weight, dtype=np.float32))

    nc = get_nc()
    in_maps = []
    for c in range(8):
        b, h = c // 2, c % 2
        xb = np.ascontiguousarray(
            np.concatenate([x[b, 2048 * h:2048 * (h + 1)],
                            x[b, 2048 * (1 - h):2048 * (2 - h)]], axis=0))
        in_maps.append({"xb": xb, "qw": q, "kw": k, "vo": vo, "ww": w})
    res = run_bass_kernel_spmd(nc, in_maps, core_ids=list(range(8)))
    out = np.empty((4, 4096, 1024), dtype=np.float32)
    for c in range(8):
        b, h = c // 2, c % 2
        out[b, 2048 * h:2048 * (h + 1)] = res.results[c]["y"]
    return out


# version 55
# speedup vs baseline: 1.0197x; 1.0197x over previous
import numpy as np
import concourse.bass as bass
import concourse.mybir as mybir
from concourse.bass_utils import run_bass_kernel_spmd
from contextlib import ExitStack

F32 = mybir.dt.float32
BF16 = mybir.dt.bfloat16
AX = mybir.AxisListType.X
AF = mybir.ActivationFunctionType
OP = mybir.AluOpType

D = 1024
R = 64
DC = 8
ENGINES = ("sync", "gpsimd", "vector", "scalar", "tensor")


class Sched:

    def __init__(self):
        self.ops = {e: [] for e in ENGINES}
        self.count = {}
        self.wmax = {e: {} for e in ENGINES}
        self.tw = {}
        self.tr = {}
        self.sems = set(ENGINES)

    def emit(self, eng, fn, reads=(), writes=(), sem=None, dma=False):
        sem = sem or eng
        self.sems.add(sem)
        deps = {}

        def add(psem, pv):
            if psem == "tensor" and eng == "tensor":
                return
            if pv > deps.get(psem, 0):
                deps[psem] = pv

        for t in reads:
            w = self.tw.get(t)
            if w is not None:
                add(*w)
        for t in writes:
            w = self.tw.get(t)
            if w is not None:
                add(*w)
            for r in self.tr.get(t, ()):
                add(*r)
        step = 16 if dma else 1
        v = self.count.get(sem, 0) + step
        self.count[sem] = v
        waits = []
        wm = self.wmax[eng]
        for psem, pv in sorted(deps.items()):
            if pv > wm.get(psem, 0):
                waits.append((psem, pv))
                wm[psem] = pv
        self.ops[eng].append((waits, fn, sem, step))
        for t in writes:
            self.tw[t] = (sem, v)
            self.tr[t] = []
        for t in reads:
            self.tr.setdefault(t, []).append((sem, v))
        return v


def build_nc(nt=32, nq=16):
    T = 128 * nt
    TQ = 128 * nq
    nc = bass.Bass("TRN2", target_bir_lowering=False)

    xb = nc.dram_tensor("xb", [T, D], F32, kind="ExternalInput").ap()
    qw = nc.dram_tensor("qw", [R, D], F32, kind="ExternalInput").ap()
    kw = nc.dram_tensor("kw", [R, D], F32, kind="ExternalInput").ap()
    vo = nc.dram_tensor("vo", [D, D], F32, kind="ExternalInput").ap()
    ww = nc.dram_tensor("ww", [D, D], F32, kind="ExternalInput").ap()
    y = nc.dram_tensor("y", [TQ, D], F32, kind="ExternalOutput").ap()

    es = ExitStack()
    with es:
        def sb(name, shape, dt=F32):
            return es.enter_context(nc.sbuf_tensor(name, shape, dt))

        def ps(name, shape, dt=F32):
            return es.enter_context(nc.psum_tensor(name, shape, dt))

        V = sb("V", [128, nt * 1024], BF16)
        VOc = sb("VOc", [128, DC * 1024], BF16)
        wqT = sb("wqT", [128, DC * 1024], BF16)
        Pbf = sb("Pbf", [128, nt * 128], BF16)
        PT = sb("PT", [128, nt * 128], BF16)
        xkT = sb("xkT", [64, nt * 128], BF16)
        xqT = sb("xqT", [64, nq * 128], BF16)
        xstg = [sb(f"xstg{p}", [128, D]) for p in range(2)]
        xres = [sb(f"xres{p}", [128, D]) for p in range(2)]
        wstg = [sb(f"wstg{p}", [128, D]) for p in range(2)]
        hb = [sb(f"hb{p}", [128, D]) for p in range(2)]
        hn = [sb(f"hn{p}", [128, D]) for p in range(2)]
        yp = [sb(f"yp{p}", [128, D]) for p in range(2)]
        sbuf_s = sb("sbuf_s", [128, D])
        cbuf = sb("cbuf", [128, D])
        aq = [sb(f"aq{p}", [128, D], BF16) for p in range(2)]
        aqT = sb("aqT", [128, D], BF16)
        outb = [sb(f"outb{p}", [128, D]) for p in range(2)]
        xnT = [sb(f"xnT{p}", [128, D], BF16) for p in range(2)]
        QKT = sb("QKT", [128, 1024], BF16)
        ident = sb("ident", [128, 128], BF16)
        ones = sb("ones", [128, 1])
        onesr = sb("onesr", [1, 128])
        asum8 = sb("asum8", [128, 8])
        easum = sb("easum", [128, 16])
        wsA = sb("wsA", [1, 4])
        wsumv = sb("wsumv", [128, 1])
        bcast2 = sb("bcast2", [128, 2])
        cb7 = sb("cb7", [128, 7])
        st1 = [sb(f"st1{p}", [128, 24]) for p in range(2)]
        st2 = [sb(f"st2{p}", [128, 32]) for p in range(2)]

        ps_s01 = ps("ps_s01", [128, 1024])
        ps_a01 = ps("ps_a01", [128, 1024])
        ps_a = [ps_a01[:, 0:512], ps_a01[:, 512:1024]]
        ps_t = [ps("ps_t0", [128, 1024], BF16), ps("ps_t1", [128, 1024], BF16)]
        ps_kq = ps("ps_kq", [128, 1024])

        S = Sched()

        S.emit("gpsimd", lambda e: e.memset(ident[:], 0.0), writes=["ident"])
        S.emit("gpsimd", lambda e: e.affine_select(
            out=ident[:], in_=ident[:], compare_op=OP.not_equal, fill=1.0,
            base=0, pattern=[[-1, 128]], channel_multiplier=1),
            reads=["ident"], writes=["ident"])
        S.emit("vector", lambda e: e.memset(ones[:], 1.0), writes=["ones"])
        S.emit("vector", lambda e: e.memset(easum[:], 0.0),
               writes=[f"easum{p}_{c}" for p in range(2) for c in range(5)])
        for k in range(7):
            S.emit("vector", lambda e, k=k: e.memset(cb7[:, k:k + 1], -float(1 << (6 - k))),
                   writes=[f"cb7_{k}"])
        S.emit("vector", lambda e: e.memset(onesr[:], 1.0), writes=["onesr"])

        S.emit("sync", lambda e: e.dma_start(out=xres[0][0:64, :], in_=qw[:]),
               writes=["xres0"], sem="dq", dma=True)
        S.emit("sync", lambda e: e.dma_start(out=xres[1][0:64, :], in_=kw[:]),
               writes=["xres1"], sem="dk", dma=True)
        S.emit("vector", lambda e: e.tensor_scalar_mul(
            out=aq[0][0:64, :], in0=xres[0][0:64, :], scalar1=1.0 / 32.0),
            reads=["xres0"], writes=["aq0"])
        S.emit("vector", lambda e: e.tensor_copy(out=aq[1][0:64, :], in_=xres[1][0:64, :]),
               reads=["xres1"], writes=["aq1"])
        for j in range(DC):
            def tpqk(e, j=j):
                return e.transpose(ps_t[0][:, 64 * j:64 * j + 64],
                                   aq[0][0:64, 128 * j:128 * j + 128], ident[0:64, 0:64])
            S.emit("tensor", tpqk, reads=["aq0", "ident"], writes=[f"ps_t0_s{j}"])
        for j in range(DC):
            def tpqk2(e, j=j):
                return e.transpose(ps_t[1][:, 64 * j:64 * j + 64],
                                   aq[1][0:64, 128 * j:128 * j + 128], ident[0:64, 0:64])
            S.emit("tensor", tpqk2, reads=["aq1", "ident"], writes=[f"ps_t1_s{j}"])
        S.emit("vector", lambda e: e.tensor_copy(out=QKT[:, 0:512], in_=ps_t[0][:, 0:512]),
               reads=[f"ps_t0_s{j}" for j in range(DC)], writes=["QKT"])
        S.emit("vector", lambda e: e.tensor_copy(out=QKT[:, 512:1024], in_=ps_t[1][:, 0:512]),
               reads=[f"ps_t1_s{j}" for j in range(DC)], writes=["QKT"])

        def emit_vo_chunk(j):
            b = j % 4
            stg = [wstg[0], wstg[1], hb[0], hb[1]][b]
            wtok = [["wstg0"], ["wstg1"], ["hb0_0", "hb0_1"], ["hb1_0", "hb1_1"]][b]
            def ldvo(e, j=j, stg=stg):
                return e.dma_start(out=stg[:], in_=vo[128 * j:128 * (j + 1), :])
            S.emit("sync", ldvo, writes=wtok, sem=["dw0", "dw1", "dv0", "dv1"][b], dma=True)
            if j % 2 == 0:
                def cvo(e, j=j, stg=stg):
                    return e.copy(out=VOc[:, 1024 * j:1024 * (j + 1)], in_=stg[:])
                S.emit("scalar", cvo, reads=wtok, writes=[f"VOc{j}"])
            else:
                def cvo(e, j=j, stg=stg):
                    return e.tensor_copy(out=VOc[:, 1024 * j:1024 * (j + 1)], in_=stg[:])
                S.emit("vector", cvo, reads=wtok, writes=[f"VOc{j}"])

        def emit_vo_weights():
            for j in range(DC):
                emit_vo_chunk(j)

        def wabs_chunk(j):
            def ldw(e, j=j):
                return e.dma_start(out=wstg[j % 2][:], in_=ww[128 * j:128 * (j + 1), :])
            S.emit("sync", ldw, writes=[f"wstg{j % 2}"], sem=f"dw{j % 2}", dma=True)
            def absw(e, j=j):
                return e.activation(out=cbuf[:], in_=wstg[j % 2][:], func=AF.Abs,
                                    accum_out=asum8[:, j:j + 1])
            S.emit("scalar", absw, reads=[f"wstg{j % 2}"], writes=["cbuf", f"asum8_{j}"])

        def wscale_finish():
            S.emit("vector", lambda e: e.reduce_sum(out=wsumv[:], in_=asum8[:], axis=AX),
                   reads=[f"asum8_{j}" for j in range(DC)], writes=["wsum"])
            S.emit("tensor", lambda e: e.matmul(ps_kq[0:1, 0:1], wsumv[:], ones[:],
                                                start=True, stop=True),
                   reads=["wsum", "ones"], writes=["ps_kq_0"])
            S.emit("scalar", lambda e: e.activation(out=wsA[0:1, 0:1], in_=ps_kq[0:1, 0:1],
                                                    func=AF.Copy, bias=1e-8,
                                                    scale=1.0 / (D * D)),
                   reads=["ps_kq_0"], writes=["ws"])
            S.emit("vector", lambda e: e.reciprocal(out=wsA[0:1, 1:2], in_=wsA[0:1, 0:1]),
                   reads=["ws"], writes=["rws"])
            S.emit("scalar", lambda e: e.mul(out=wsA[0:1, 2:3], in_=wsA[0:1, 0:1],
                                             mul=1.0 / 127.0),
                   reads=["ws"], writes=["ws127"])
            S.emit("tensor", lambda e: e.matmul(ps_s01[:, 0:2], onesr[0:1, 0:128],
                                                wsA[0:1, 1:3], start=True, stop=True),
                   reads=["rws", "ws127", "onesr"], writes=["ps_s01_0"])
            S.emit("vector", lambda e: e.tensor_copy(out=bcast2[:], in_=ps_s01[:, 0:2]),
                   reads=["ps_s01_0"], writes=["bcast2"])

        def wquant_v(co):
            def ldw2(e, co=co):
                return e.dma_start(out=wstg[co % 2][:], in_=ww[128 * co:128 * (co + 1), :])
            S.emit("sync", ldw2, writes=[f"wstg{co % 2}"], sem=f"dw{co % 2}", dma=True)
            def uop(e, co=co):
                return e.tensor_scalar_mul(out=cbuf[:], in0=wstg[co % 2][:],
                                           scalar1=bcast2[:, 0:1])
            S.emit("vector", uop, reads=[f"wstg{co % 2}", "bcast2"], writes=["cbuf"])
            S.emit("vector", lambda e: e.tensor_scalar(
                out=sbuf_s[:], in0=cbuf[:], scalar1=0.5, scalar2=None, op0=OP.is_ge),
                reads=["cbuf"], writes=["s_a", "s_b"])
            S.emit("vector", lambda e: e.tensor_scalar(
                out=hb[0][:], in0=cbuf[:], scalar1=-0.5, scalar2=None, op0=OP.is_le),
                reads=["cbuf"], writes=["hb0_0", "hb0_1"])
            S.emit("vector", lambda e: e.tensor_tensor(
                out=aqT[:], in0=sbuf_s[:], in1=hb[0][:], op=OP.subtract),
                reads=["s_a", "s_b", "hb0_0", "hb0_1"], writes=["aqT"])

        def wquant_tp(co):
            parw = co % 2
            for ci in range(DC):
                def tpw(e, co=co, ci=ci, parw=parw):
                    return e.transpose(ps_t[parw][:, 128 * ci:128 * ci + 128],
                                       aqT[:, 128 * ci:128 * ci + 128], ident[:])
                S.emit("tensor", tpw, reads=["aqT", "ident"],
                       writes=[f"ps_t{parw}_s{ci}"])
            def evw(e, co=co, parw=parw):
                src = ps_t[parw][:].rearrange("p (ci f) -> p ci f", f=128)
                dst = wqT[:].rearrange("p (ci f) -> p ci f", f=1024)[:, :, 128 * co:128 * co + 128]
                return e.copy(out=dst, in_=src)
            S.emit("scalar", evw, reads=[f"ps_t{parw}_s{ci}" for ci in range(DC)],
                   writes=["wqT"])

        def wquant_chunk(co):
            wquant_v(co)
            wquant_tp(co)

        def p1_load(i):
            par = i % 2
            def ldx(e, i=i, par=par):
                return e.dma_start(out=xstg[par][:], in_=xb[128 * i:128 * (i + 1), :])
            S.emit("sync", ldx, writes=[f"xstg{par}"], sem=f"dx{par}", dma=True)

        def p1_stats(i):
            par = i % 2
            stp = st1[par]
            stok = [f"st1{par}_{k}" for k in range(18)]
            xt = f"xstg{par}"
            S.emit("vector", lambda e, par=par, stp=stp: e.bn_stats(
                out=stp[:, 0:6], in_=xstg[par][:, 0:512]),
                reads=[xt], writes=[stok[0]])
            S.emit("vector", lambda e, par=par, stp=stp: e.bn_stats(
                out=stp[:, 6:12], in_=xstg[par][:, 512:1024]),
                reads=[xt], writes=[stok[1]])
            S.emit("vector", lambda e, stp=stp: e.bn_aggr(
                out=stp[:, 12:14], in_=stp[:, 0:12]),
                reads=[stok[0], stok[1]], writes=[stok[12]])
            S.emit("vector", lambda e, stp=stp: e.tensor_scalar_mul(
                out=stp[:, 14:15], in0=stp[:, 12:13], scalar1=-1.0),
                reads=[stok[12]], writes=[stok[14]])
            S.emit("vector", lambda e, stp=stp: e.tensor_scalar_add(
                out=stp[:, 15:16], in0=stp[:, 13:14], scalar1=1e-5),
                reads=[stok[12]], writes=[stok[15]])
            S.emit("scalar", lambda e, stp=stp: e.activation(out=stp[:, 16:17], in_=stp[:, 15:16], func=AF.Ln),
                   reads=[stok[15]], writes=[stok[16]])
            S.emit("scalar", lambda e, stp=stp: e.activation(out=stp[:, 17:18], in_=stp[:, 16:17], func=AF.Exp, scale=-0.5),
                   reads=[stok[16]], writes=[stok[17]])

        def p1_xn(i):
            par = i % 2
            stp = st1[par]
            stok = [f"st1{par}_{k}" for k in range(18)]
            xt = f"xstg{par}"
            def xnop2(e, par=par, stp=stp):
                return e.tensor_scalar(out=aq[par][:], in0=xstg[par][:],
                                       scalar1=stp[:, 14:15], scalar2=stp[:, 17:18],
                                       op0=OP.add, op1=OP.mult)
            S.emit("vector", xnop2, reads=[xt, stok[14], stok[17]],
                   writes=[f"aq{par}", f"aq{par}_v", f"aq{par}_g"])
            for j in range(DC):
                def tpx(e, j=j, par=par):
                    return e.transpose(ps_t[par][:, 128 * j:128 * j + 128],
                                       aq[par][:, 128 * j:128 * j + 128], ident[:])
                S.emit("tensor", tpx, reads=[f"aq{par}", f"aq{par}_v", f"aq{par}_g", "ident"],
                       writes=[f"ps_t{par}_s{j}"])
            def evx(e, par=par):
                return e.copy(out=xnT[par][:], in_=ps_t[par][:])
            S.emit("scalar", evx, reads=[f"ps_t{par}_s{j}" for j in range(DC)],
                   writes=[f"xnT{par}"])

        def p1_mm(i):
            par = i % 2
            for half in range(2):
                def vmm(e, par=par, half=half):
                    last = None
                    for j in range(4 * half, 4 * half + 4):
                        for eh in range(2):
                            last = e.matmul(ps_a[eh],
                                            xnT[par][:, 128 * j:128 * j + 128],
                                            VOc[:, 1024 * j + 512 * eh:1024 * j + 512 * eh + 512],
                                            start=(j == 0), stop=(j == DC - 1))
                    return last
                S.emit("tensor", vmm,
                       reads=[f"xnT{par}"] + [f"VOc{j}" for j in range(4 * half, 4 * half + 4)],
                       writes=["ps_a0", "ps_a1"])
            for eh in range(2):
                def evv(e, i=i, eh=eh):
                    return e.copy(out=V[:, 1024 * i + 512 * eh:1024 * i + 512 * eh + 512],
                                  in_=ps_a[eh])
                S.emit("scalar", evv, reads=[f"ps_a{eh}"], writes=[f"V{i}e{eh}"])
            def kmm(e, par=par):
                last = None
                for j in range(DC):
                    last = e.matmul(ps_kq[0:64, 0:128],
                                    QKT[:, 512 + 64 * j:512 + 64 * j + 64],
                                    xnT[par][:, 128 * j:128 * j + 128],
                                    start=(j == 0), stop=(j == DC - 1))
                return last
            S.emit("tensor", kmm, reads=[f"xnT{par}", "QKT"], writes=["ps_kq_0"])
            def evk(e, i=i):
                return e.copy(out=xkT[0:64, 128 * i:128 * i + 128],
                              in_=ps_kq[0:64, 0:128])
            S.emit("scalar", evk, reads=["ps_kq_0"], writes=[f"xk{i}"])
            if i < nq:
                def qmm(e, par=par):
                    last = None
                    for j in range(DC):
                        last = e.matmul(ps_kq[0:64, 512:640],
                                        QKT[:, 64 * j:64 * j + 64],
                                        xnT[par][:, 128 * j:128 * j + 128],
                                        start=(j == 0), stop=(j == DC - 1))
                    return last
                S.emit("tensor", qmm, reads=[f"xnT{par}", "QKT"], writes=["ps_kq_1"])
                def evq(e, i=i):
                    return e.copy(out=xqT[0:64, 128 * i:128 * i + 128],
                                  in_=ps_kq[0:64, 512:640])
                S.emit("scalar", evq, reads=["ps_kq_1"], writes=[f"xq{i}"])

        p1_load(0)
        p1_stats(0)
        if nt > 1:
            p1_load(1)
        interleave = nt >= 28
        if not interleave:
            emit_vo_weights()
        for i in range(nt):
            p1_xn(i)
            if interleave and i >= 17 and i % 2 == 1 and (i - 17) // 2 < DC:
                wquant_v((i - 17) // 2)
            if i + 2 < nt:
                p1_load(i + 2)
            if i + 1 < nt:
                p1_stats(i + 1)
            if interleave and i == 0:
                for j in range(DC):
                    emit_vo_chunk(j)
            if i >= 1:
                p1_mm(i - 1)
            if interleave:
                if 8 <= i < 8 + DC:
                    wabs_chunk(i - 8)
                if i == 8 + DC:
                    wscale_finish()
                if i >= 17 and i % 2 == 1 and (i - 17) // 2 < DC:
                    wquant_tp((i - 17) // 2)
        p1_mm(nt - 1)
        if not interleave:
            for j in range(DC):
                wabs_chunk(j)
            wscale_finish()
            for co in range(DC):
                wquant_chunk(co)

        NKC = nt // 4
        ng = nt // 8

        def stA1a(t):
            par = t % 2
            def ldr(e, t=t, par=par):
                return e.dma_start(out=xres[par][:], in_=xb[128 * t:128 * (t + 1), :])
            S.emit("sync", ldr, writes=[f"xres{par}"], sem=f"dr{par}", dma=True)
            def smm0(e, t=t):
                return e.matmul(ps_s01[:, 0:512], xqT[0:64, 128 * t:128 * t + 128],
                                xkT[0:64, 0:512], start=True, stop=True)
            S.emit("tensor", smm0,
                   reads=[f"xq{t}"] + [f"xk{l}" for l in range(4)],
                   writes=["ps_s01_0"])

        def stA1ne(t):
            par = t % 2
            stp = st2[par]
            stok = [f"st2{par}_{k}" for k in range(28)]
            S.emit("vector", lambda e, stp=stp: e.reduce_max(
                out=stp[:, 0:1], in_=ps_s01[:, 0:128], axis=AX, negate=True),
                reads=["ps_s01_0"], writes=[stok[0]])
            for hh in range(2):
                def expp(e, par=par, stp=stp, hh=hh):
                    return e.activation(
                        out=Pbf[:, 512 * hh:512 * hh + 512],
                        in_=ps_s01[:, 512 * hh:512 * hh + 512], func=AF.Exp,
                        bias=stp[:, 0:1],
                        accum_out=easum[:, 8 * par + 4 * hh:8 * par + 4 * hh + 1])
                S.emit("scalar", expp,
                       reads=[f"ps_s01_{hh}", stok[0]],
                       writes=[f"P{hh}", f"easum{par}_{4 * hh}"])

        def _a1b_chunk(t, c):
            par = t % 2
            stp = st2[par]
            stok = [f"st2{par}_{k}" for k in range(28)]
            def sbank(c):
                pa = c in (4, 5)
                tens = ps_a01 if pa else ps_s01
                off = 512 * (c % 2)
                return tens[:, off:off + 512], ("ps_a" if pa else "ps_s01_") + str(c % 2)
            if c > 0:
                def smm(e, t=t, c=c):
                    ap, _ = sbank(c)
                    return e.matmul(ap, xqT[0:64, 128 * t:128 * t + 128],
                                    xkT[0:64, 512 * c:512 * c + 512],
                                    start=True, stop=True)
                S.emit("tensor", smm,
                       reads=[f"xq{t}"] + [f"xk{4 * c + l}" for l in range(4)],
                       writes=[sbank(c)[1]])
            if c % 2 == 1 and c > 1:
                cp = c // 2
                pa = c in (4, 5)
                tens = ps_a01 if pa else ps_s01
                ttok = "ps_a" if pa else "ps_s01"
                def expp(e, cp=cp, par=par, stp=stp, tens=tens):
                    return e.activation(out=Pbf[:, 1024 * cp:1024 * cp + 1024],
                                        in_=tens[:], func=AF.Exp,
                                        bias=stp[:, 0:1],
                                        accum_out=easum[:, 8 * par + cp:8 * par + cp + 1])
                rd0 = ttok + ("0" if ttok == "ps_a" else "_0")
                rd1 = ttok + ("1" if ttok == "ps_a" else "_1")
                S.emit("scalar", expp,
                       reads=[rd0, rd1, stok[0]],
                       writes=[f"P{2 * cp}", f"P{2 * cp + 1}", f"easum{par}_{cp}"])

        def stA1b0(t):
            for c in range(min(2, NKC)):
                _a1b_chunk(t, c)

        def stA1b1(t):
            _tp_group(t, 0)
            for c in range(2, NKC):
                _a1b_chunk(t, c)

        def _tp_group(t, g):
            gp = g % 2
            for sl in range(8):
                kb = 8 * g + sl
                def tpp(e, kb=kb, sl=sl, gp=gp):
                    return e.transpose(ps_t[gp][:, 128 * sl:128 * sl + 128],
                                       Pbf[:, 128 * kb:128 * kb + 128], ident[:])
                S.emit("tensor", tpp, reads=[f"P{kb // 4}", "ident"],
                       writes=[f"ps_t{gp}_s{sl}"])
            def evp(e, g=g, gp=gp):
                return e.copy(out=PT[:, 1024 * g:1024 * (g + 1)], in_=ps_t[gp][:])
            S.emit("scalar", evp, reads=[f"ps_t{gp}_s{sl}" for sl in range(8)],
                   writes=[f"PT{g}"])

        def _pv_group(t, g, eh):
            def pvmm(e, g=g, eh=eh):
                last = None
                for sl in range(8):
                    kb = 8 * g + sl
                    last = e.matmul(
                        ps_a[eh], PT[:, 128 * kb:128 * kb + 128],
                        V[:, 1024 * kb + 512 * eh:1024 * kb + 512 * eh + 512],
                        start=(kb == 0), stop=(kb == nt - 1))
                return last
            S.emit("tensor", pvmm,
                   reads=[f"PT{g}"] + [f"V{8 * g + sl}e{eh}" for sl in range(8)],
                   writes=[f"ps_a{eh}"])

        def stB_tp(t):
            for g in range(1, ng):
                _tp_group(t, g)
                _pv_group(t, g - 1, 0)
            _pv_group(t, ng - 1, 0)

        def stB_pv(t, eh):
            if eh == 0:
                return
            for g in range(ng):
                _pv_group(t, g, eh)

        def stC1r(t):
            par = t % 2
            stp = st2[par]
            stok = [f"st2{par}_{k}" for k in range(28)]
            S.emit("vector", lambda e, stp=stp, par=par: e.reduce_sum(
                out=stp[:, 1:2], in_=easum[:, 8 * par:8 * par + 5], axis=AX),
                   reads=[f"easum{par}_{c}" for c in range(NKC // 2)]
                         + [f"easum{par}_4"], writes=[stok[1]])
            S.emit("vector", lambda e, stp=stp: e.reciprocal(out=stp[:, 2:3], in_=stp[:, 1:2]),
                   reads=[stok[1]], writes=[stok[2]])

        def stC1h(t, eh):
            par = t % 2
            stp = st2[par]
            stok = [f"st2{par}_{k}" for k in range(28)]
            def hop(e, eh=eh, par=par, stp=stp):
                return e.scalar_tensor_tensor(
                    out=hb[par][:, 512 * eh:512 * eh + 512], in0=ps_a[eh],
                    scalar=stp[:, 2:3], in1=xres[par][:, 512 * eh:512 * eh + 512],
                    op0=OP.mult, op1=OP.add)
            S.emit("vector", hop, reads=[f"ps_a{eh}", stok[2], f"xres{par}"],
                   writes=[f"hb{par}_{eh}"])
            def bno(e, eh=eh, par=par, stp=stp):
                return e.bn_stats(out=stp[:, 3 + 6 * eh:9 + 6 * eh],
                                  in_=hb[par][:, 512 * eh:512 * eh + 512])
            S.emit("vector", bno, reads=[f"hb{par}_{eh}"], writes=[stok[3 + eh]])

        def stC1a2(t):
            par = t % 2
            stp = st2[par]
            stok = [f"st2{par}_{k}" for k in range(28)]
            S.emit("vector", lambda e, stp=stp: e.bn_aggr(
                out=stp[:, 15:17], in_=stp[:, 3:15]),
                reads=[stok[3], stok[4]], writes=[stok[15]])
            S.emit("vector", lambda e, stp=stp: e.tensor_scalar_add(
                out=stp[:, 18:19], in0=stp[:, 16:17], scalar1=1e-5),
                reads=[stok[15]], writes=[stok[18]])
            S.emit("vector", lambda e, stp=stp: e.tensor_scalar_mul(
                out=stp[:, 17:18], in0=stp[:, 15:16], scalar1=-1.0),
                reads=[stok[15]], writes=[stok[17]])
            S.emit("scalar", lambda e, stp=stp: e.activation(out=stp[:, 19:20], in_=stp[:, 18:19], func=AF.Ln),
                   reads=[stok[18]], writes=[stok[19]])
            S.emit("scalar", lambda e, stp=stp: e.activation(out=stp[:, 20:21], in_=stp[:, 19:20], func=AF.Exp, scale=-0.5),
                   reads=[stok[19]], writes=[stok[20]])

        def stC1b_a(t):
            par = t % 2
            stp = st2[par]
            stok = [f"st2{par}_{k}" for k in range(28)]
            def uop2(e, par=par, stp=stp):
                return e.tensor_scalar_add(out=hn[par][:], in0=hb[par][:],
                                           scalar1=stp[:, 17:18])
            S.emit("vector", uop2, reads=[f"hb{par}_0", f"hb{par}_1", stok[17]],
                   writes=[f"hn{par}"])
            S.emit("vector", lambda e, par=par, stp=stp: e.reduce_max(
                out=stp[:, 21:22], in_=hn[par][:], axis=AX, apply_absolute_value=True),
                reads=[f"hn{par}"], writes=[stok[21]])

        def stC1b_b(t):
            par = t % 2
            stp = st2[par]
            stok = [f"st2{par}_{k}" for k in range(28)]
            S.emit("vector", lambda e, stp=stp: e.tensor_scalar(
                out=stp[:, 22:23], in0=stp[:, 21:22], scalar1=stp[:, 20:21],
                scalar2=1e-8, op0=OP.mult, op1=OP.add),
                reads=[stok[21], stok[20]], writes=[stok[22]])
            S.emit("vector", lambda e, stp=stp: e.reciprocal(out=stp[:, 26:27], in_=stp[:, 22:23]),
                   reads=[stok[22]], writes=[stok[26]])
            S.emit("vector", lambda e, stp=stp: e.tensor_scalar(
                out=stp[:, 23:24], in0=stp[:, 26:27], scalar1=stp[:, 20:21],
                scalar2=127.0, op0=OP.mult, op1=OP.mult),
                reads=[stok[26], stok[20]], writes=[stok[23]])
            def ypop(e, par=par, stp=stp):
                return e.tensor_scalar(out=yp[par][:], in0=hn[par][:],
                                       scalar1=stp[:, 23:24], scalar2=0.5,
                                       op0=OP.mult, op1=OP.add)
            S.emit("vector", ypop, reads=[f"hn{par}", stok[23]], writes=[f"yp{par}"])
            S.emit("vector", lambda e, stp=stp: e.tensor_tensor(
                out=stp[:, 24:25], in0=stp[:, 22:23], in1=bcast2[:, 1:2], op=OP.mult),
                reads=[stok[22], "bcast2"], writes=[stok[24]])

        def stC2(t):
            par = t % 2
            S.emit("vector", lambda e, par=par: e.tensor_scalar(
                out=cbuf[:], in0=yp[par][:], scalar1=0.0, scalar2=128.0,
                op0=OP.is_lt, op1=OP.mult),
                reads=[f"yp{par}"], writes=["cbuf"])
            S.emit("vector", lambda e, par=par: e.tensor_tensor(
                out=sbuf_s[:], in0=yp[par][:], in1=cbuf[:], op=OP.add),
                reads=[f"yp{par}", "cbuf"], writes=["s_a", "s_b"])
            for k in range(6, -1, -1):
                tw = float(1 << k)
                S.emit("vector", lambda e, tw=tw: e.tensor_scalar(
                    out=cbuf[:], in0=sbuf_s[:], scalar1=tw, scalar2=tw,
                    op0=OP.is_ge, op1=OP.mult),
                    reads=["s_a", "s_b"], writes=["cbuf"])
                S.emit("vector", lambda e: e.tensor_tensor(
                    out=sbuf_s[:], in0=sbuf_s[:], in1=cbuf[:], op=OP.subtract),
                    reads=["s_a", "s_b", "cbuf"], writes=["s_a", "s_b"])
            S.emit("vector", lambda e, par=par: e.scalar_tensor_tensor(
                out=aq[par][:], in0=sbuf_s[:], scalar=-1.0, in1=yp[par][:],
                op0=OP.mult, op1=OP.add),
                reads=["s_a", "s_b", f"yp{par}"],
                writes=[f"aq{par}", f"aq{par}_v", f"aq{par}_g"])

        def stD_tp(t):
            par = t % 2
            for j in range(DC):
                h, sl = divmod(j, 4)
                def tpa(e, j=j, h=h, sl=sl, par=par):
                    return e.transpose(ps_t[h][:, 128 * sl:128 * sl + 128],
                                       aq[par][:, 128 * j:128 * j + 128], ident[:])
                S.emit("tensor", tpa, reads=[f"aq{par}_v", f"aq{par}_g", "ident"],
                       writes=[f"ps_t{h}_s{sl}"])
            for h in range(2):
                def eva(e, h=h):
                    return e.copy(out=aqT[:, 512 * h:512 * h + 512],
                                  in_=ps_t[h][:, 0:512])
                S.emit("scalar", eva,
                       reads=[f"ps_t{h}_s{sl}" for sl in range(4)],
                       writes=[f"aqT_h{h}"])

        def stD_ffn(t):
            def fmm(e, j0, j1):
                last = None
                for j in range(j0, j1):
                    for oh in range(2):
                        last = e.matmul(
                            ps_kq[:, 512 * oh:512 * oh + 512],
                            aqT[:, 128 * j:128 * j + 128],
                            wqT[:, 1024 * j + 512 * oh:1024 * j + 512 * oh + 512],
                            start=(j == 0), stop=(j == DC - 1))
                return last
            S.emit("tensor", lambda e: fmm(e, 0, DC // 2),
                   reads=["aqT_h0", "wqT"], writes=["ps_kq_0", "ps_kq_1"])
            S.emit("tensor", lambda e: fmm(e, DC // 2, DC),
                   reads=["aqT_h1", "wqT"], writes=["ps_kq_0", "ps_kq_1"])

        def stD_out(t):
            par = t % 2
            stp = st2[par]
            stok = [f"st2{par}_{k}" for k in range(28)]
            for eh in range(2):
                def oop(e, eh=eh, par=par, stp=stp):
                    return e.scalar_tensor_tensor(
                        out=outb[par][:, 512 * eh:512 * eh + 512],
                        in0=ps_kq[:, 512 * eh:512 * eh + 512],
                        scalar=stp[:, 24:25],
                        in1=hb[par][:, 512 * eh:512 * eh + 512],
                        op0=OP.mult, op1=OP.add)
                S.emit("vector", oop, reads=[f"ps_kq_{eh}", stok[24],
                                             f"hb{par}_{eh}"],
                       writes=[f"outb{par}_e{eh}"])
            def stx(e, t=t, par=par):
                return e.dma_start(out=y[128 * t:128 * (t + 1), :], in_=outb[par][:])
            S.emit("gpsimd", stx, reads=[f"outb{par}_e0", f"outb{par}_e1"],
                   sem=f"do{par}", dma=True)

        stA1a(0)
        stA1b0(0)
        stA1ne(0)
        stA1b1(0)
        for t in range(nq):
            stB_tp(t)
            stC1r(t)
            stC1h(t, 0)
            stB_pv(t, 1)
            if t + 1 < nq:
                stA1a(t + 1)
                stA1b0(t + 1)
            stC1h(t, 1)
            if t > 0:
                stD_tp(t - 1)
            stC1a2(t)
            if t > 0:
                stD_ffn(t - 1)
            if t + 1 < nq:
                stA1ne(t + 1)
            stC1b_a(t)
            stC1b_b(t)
            if t > 0:
                stD_out(t - 1)
            if t + 1 < nq:
                stA1b1(t + 1)
            stC2(t)
        stD_tp(nq - 1)
        stD_ffn(nq - 1)
        stD_out(nq - 1)

        sems = {name: es.enter_context(nc.semaphore(f"sem_{name}"))
                for name in sorted(S.sems)}
        block = es.enter_context(nc.Block())
        final_waits = [(sn, S.count[sn]) for sn in sorted(S.count)
                       if sn.startswith("do")]

        def section(name):
            def body(eng):
                for waits, fn, sem, step in S.ops[name]:
                    for psem, pv in waits:
                        eng.wait_ge(sems[psem], pv)
                    inst = fn(eng)
                    inst.then_inc(sems[sem], step)
                if name == "gpsimd":
                    for sn, val in final_waits:
                        eng.wait_ge(sems[sn], val)
            return body

        block.sync(section("sync"))
        block.gpsimd(section("gpsimd"))
        block.vector(section("vector"))
        block.scalar(section("scalar"))
        block.tensor(section("tensor"))
    return nc


_NC_CACHE = {}


def get_nc(nt=32, nq=16):
    key = (nt, nq)
    if key not in _NC_CACHE:
        _NC_CACHE[key] = build_nc(nt, nq)
    return _NC_CACHE[key]


def kernel(L0_attn_x, L0_attn_Q, L0_attn_K, L0_attn_VO, L0_ffn_weight):
    x = np.ascontiguousarray(np.asarray(L0_attn_x, dtype=np.float32))
    q = np.ascontiguousarray(np.asarray(L0_attn_Q, dtype=np.float32))
    k = np.ascontiguousarray(np.asarray(L0_attn_K, dtype=np.float32))
    vo = np.ascontiguousarray(np.asarray(L0_attn_VO, dtype=np.float32))
    w = np.ascontiguousarray(np.asarray(L0_ffn_weight, dtype=np.float32))

    nc = get_nc()
    in_maps = []
    for c in range(8):
        b, h = c // 2, c % 2
        xb = np.ascontiguousarray(
            np.concatenate([x[b, 2048 * h:2048 * (h + 1)],
                            x[b, 2048 * (1 - h):2048 * (2 - h)]], axis=0))
        in_maps.append({"xb": xb, "qw": q, "kw": k, "vo": vo, "ww": w})
    res = run_bass_kernel_spmd(nc, in_maps, core_ids=list(range(8)))
    out = np.empty((4, 4096, 1024), dtype=np.float32)
    for c in range(8):
        b, h = c // 2, c % 2
        out[b, 2048 * h:2048 * (h + 1)] = res.results[c]["y"]
    return out


# version 56
# speedup vs baseline: 1.0341x; 1.0142x over previous
import numpy as np
import concourse.bass as bass
import concourse.mybir as mybir
from concourse.bass_utils import run_bass_kernel_spmd
from contextlib import ExitStack

F32 = mybir.dt.float32
BF16 = mybir.dt.bfloat16
AX = mybir.AxisListType.X
AF = mybir.ActivationFunctionType
OP = mybir.AluOpType

D = 1024
R = 64
DC = 8
ENGINES = ("sync", "gpsimd", "vector", "scalar", "tensor")


class Sched:

    def __init__(self):
        self.ops = {e: [] for e in ENGINES}
        self.count = {}
        self.wmax = {e: {} for e in ENGINES}
        self.tw = {}
        self.tr = {}
        self.sems = set(ENGINES)

    def emit(self, eng, fn, reads=(), writes=(), sem=None, dma=False):
        sem = sem or eng
        self.sems.add(sem)
        deps = {}

        def add(psem, pv):
            if psem == "tensor" and eng == "tensor":
                return
            if pv > deps.get(psem, 0):
                deps[psem] = pv

        for t in reads:
            w = self.tw.get(t)
            if w is not None:
                add(*w)
        for t in writes:
            w = self.tw.get(t)
            if w is not None:
                add(*w)
            for r in self.tr.get(t, ()):
                add(*r)
        step = 16 if dma else 1
        v = self.count.get(sem, 0) + step
        self.count[sem] = v
        waits = []
        wm = self.wmax[eng]
        for psem, pv in sorted(deps.items()):
            if pv > wm.get(psem, 0):
                waits.append((psem, pv))
                wm[psem] = pv
        self.ops[eng].append((waits, fn, sem, step))
        for t in writes:
            self.tw[t] = (sem, v)
            self.tr[t] = []
        for t in reads:
            self.tr.setdefault(t, []).append((sem, v))
        return v


def build_nc(nt=32, nq=16):
    T = 128 * nt
    TQ = 128 * nq
    nc = bass.Bass("TRN2", target_bir_lowering=False)

    xb = nc.dram_tensor("xb", [T, D], F32, kind="ExternalInput").ap()
    qw = nc.dram_tensor("qw", [R, D], F32, kind="ExternalInput").ap()
    kw = nc.dram_tensor("kw", [R, D], F32, kind="ExternalInput").ap()
    vo = nc.dram_tensor("vo", [D, D], F32, kind="ExternalInput").ap()
    ww = nc.dram_tensor("ww", [D, D], F32, kind="ExternalInput").ap()
    y = nc.dram_tensor("y", [TQ, D], F32, kind="ExternalOutput").ap()

    es = ExitStack()
    with es:
        def sb(name, shape, dt=F32):
            return es.enter_context(nc.sbuf_tensor(name, shape, dt))

        def ps(name, shape, dt=F32):
            return es.enter_context(nc.psum_tensor(name, shape, dt))

        V = sb("V", [128, nt * 1024], BF16)
        VOc = sb("VOc", [128, DC * 1024], BF16)
        wqT = sb("wqT", [128, DC * 1024], BF16)
        Pbf = sb("Pbf", [128, nt * 128], BF16)
        PT = sb("PT", [128, nt * 128], BF16)
        xkT = sb("xkT", [64, nt * 128], BF16)
        xqT = sb("xqT", [64, nq * 128], BF16)
        xstg = [sb(f"xstg{p}", [128, D]) for p in range(2)]
        xres = [sb(f"xres{p}", [128, D]) for p in range(2)]
        wstg = [sb(f"wstg{p}", [128, D]) for p in range(2)]
        hb = [sb(f"hb{p}", [128, D]) for p in range(2)]
        hn = [sb(f"hn{p}", [128, D]) for p in range(2)]
        yp = [sb(f"yp{p}", [128, D]) for p in range(2)]
        sbuf_s = sb("sbuf_s", [128, D])
        cbuf = sb("cbuf", [128, D])
        aq = [sb(f"aq{p}", [128, D], BF16) for p in range(2)]
        aqT = sb("aqT", [128, D], BF16)
        outb = [sb(f"outb{p}", [128, D]) for p in range(2)]
        xnT = [sb(f"xnT{p}", [128, D], BF16) for p in range(2)]
        QKT = sb("QKT", [128, 1024], BF16)
        ident = sb("ident", [128, 128], BF16)
        ones = sb("ones", [128, 1])
        onesr = sb("onesr", [1, 128])
        asum8 = sb("asum8", [128, 8])
        easum = sb("easum", [128, 16])
        wsA = sb("wsA", [1, 4])
        wsumv = sb("wsumv", [128, 1])
        bcast2 = sb("bcast2", [128, 2])
        cb7 = sb("cb7", [128, 7])
        st1 = [sb(f"st1{p}", [128, 24]) for p in range(2)]
        st2 = [sb(f"st2{p}", [128, 32]) for p in range(2)]

        ps_s01 = ps("ps_s01", [128, 1024])
        ps_a01 = ps("ps_a01", [128, 1024])
        ps_a = [ps_a01[:, 0:512], ps_a01[:, 512:1024]]
        ps_t = [ps("ps_t0", [128, 1024], BF16), ps("ps_t1", [128, 1024], BF16)]
        ps_kq = ps("ps_kq", [128, 1024])

        S = Sched()

        S.emit("gpsimd", lambda e: e.memset(ident[:], 0.0), writes=["ident"])
        S.emit("gpsimd", lambda e: e.affine_select(
            out=ident[:], in_=ident[:], compare_op=OP.not_equal, fill=1.0,
            base=0, pattern=[[-1, 128]], channel_multiplier=1),
            reads=["ident"], writes=["ident"])
        S.emit("vector", lambda e: e.memset(ones[:], 1.0), writes=["ones"])
        S.emit("vector", lambda e: e.memset(easum[:], 0.0),
               writes=[f"easum{p}_{c}" for p in range(2) for c in range(5)])
        for p in range(2):
            S.emit("vector", lambda e, p=p: e.memset(st2[p][:, 0:1], 0.0),
                   writes=[f"st2{p}_0"])
        for k in range(7):
            S.emit("vector", lambda e, k=k: e.memset(cb7[:, k:k + 1], -float(1 << (6 - k))),
                   writes=[f"cb7_{k}"])
        S.emit("vector", lambda e: e.memset(onesr[:], 1.0), writes=["onesr"])

        S.emit("sync", lambda e: e.dma_start(out=xres[0][0:64, :], in_=qw[:]),
               writes=["xres0"], sem="dq", dma=True)
        S.emit("sync", lambda e: e.dma_start(out=xres[1][0:64, :], in_=kw[:]),
               writes=["xres1"], sem="dk", dma=True)
        S.emit("vector", lambda e: e.tensor_scalar_mul(
            out=aq[0][0:64, :], in0=xres[0][0:64, :], scalar1=1.0 / 32.0),
            reads=["xres0"], writes=["aq0"])
        S.emit("vector", lambda e: e.tensor_copy(out=aq[1][0:64, :], in_=xres[1][0:64, :]),
               reads=["xres1"], writes=["aq1"])
        for j in range(DC):
            def tpqk(e, j=j):
                return e.transpose(ps_t[0][:, 64 * j:64 * j + 64],
                                   aq[0][0:64, 128 * j:128 * j + 128], ident[0:64, 0:64])
            S.emit("tensor", tpqk, reads=["aq0", "ident"], writes=[f"ps_t0_s{j}"])
        for j in range(DC):
            def tpqk2(e, j=j):
                return e.transpose(ps_t[1][:, 64 * j:64 * j + 64],
                                   aq[1][0:64, 128 * j:128 * j + 128], ident[0:64, 0:64])
            S.emit("tensor", tpqk2, reads=["aq1", "ident"], writes=[f"ps_t1_s{j}"])
        S.emit("vector", lambda e: e.tensor_copy(out=QKT[:, 0:512], in_=ps_t[0][:, 0:512]),
               reads=[f"ps_t0_s{j}" for j in range(DC)], writes=["QKT"])
        S.emit("vector", lambda e: e.tensor_copy(out=QKT[:, 512:1024], in_=ps_t[1][:, 0:512]),
               reads=[f"ps_t1_s{j}" for j in range(DC)], writes=["QKT"])

        def emit_vo_chunk(j):
            b = j % 4
            stg = [wstg[0], wstg[1], hb[0], hb[1]][b]
            wtok = [["wstg0"], ["wstg1"], ["hb0_0", "hb0_1"], ["hb1_0", "hb1_1"]][b]
            def ldvo(e, j=j, stg=stg):
                return e.dma_start(out=stg[:], in_=vo[128 * j:128 * (j + 1), :])
            S.emit("sync", ldvo, writes=wtok, sem=["dw0", "dw1", "dv0", "dv1"][b], dma=True)
            if j % 2 == 0:
                def cvo(e, j=j, stg=stg):
                    return e.copy(out=VOc[:, 1024 * j:1024 * (j + 1)], in_=stg[:])
                S.emit("scalar", cvo, reads=wtok, writes=[f"VOc{j}"])
            else:
                def cvo(e, j=j, stg=stg):
                    return e.tensor_copy(out=VOc[:, 1024 * j:1024 * (j + 1)], in_=stg[:])
                S.emit("vector", cvo, reads=wtok, writes=[f"VOc{j}"])

        def emit_vo_weights():
            for j in range(DC):
                emit_vo_chunk(j)

        def wabs_chunk(j):
            def ldw(e, j=j):
                return e.dma_start(out=wstg[j % 2][:], in_=ww[128 * j:128 * (j + 1), :])
            S.emit("sync", ldw, writes=[f"wstg{j % 2}"], sem=f"dw{j % 2}", dma=True)
            def absw(e, j=j):
                return e.activation(out=cbuf[:], in_=wstg[j % 2][:], func=AF.Abs,
                                    accum_out=asum8[:, j:j + 1])
            S.emit("scalar", absw, reads=[f"wstg{j % 2}"], writes=["cbuf", f"asum8_{j}"])

        def wscale_finish():
            S.emit("vector", lambda e: e.reduce_sum(out=wsumv[:], in_=asum8[:], axis=AX),
                   reads=[f"asum8_{j}" for j in range(DC)], writes=["wsum"])
            S.emit("tensor", lambda e: e.matmul(ps_kq[0:1, 0:1], wsumv[:], ones[:],
                                                start=True, stop=True),
                   reads=["wsum", "ones"], writes=["ps_kq_0"])
            S.emit("scalar", lambda e: e.activation(out=wsA[0:1, 0:1], in_=ps_kq[0:1, 0:1],
                                                    func=AF.Copy, bias=1e-8,
                                                    scale=1.0 / (D * D)),
                   reads=["ps_kq_0"], writes=["ws"])
            S.emit("vector", lambda e: e.reciprocal(out=wsA[0:1, 1:2], in_=wsA[0:1, 0:1]),
                   reads=["ws"], writes=["rws"])
            S.emit("scalar", lambda e: e.mul(out=wsA[0:1, 2:3], in_=wsA[0:1, 0:1],
                                             mul=1.0 / 127.0),
                   reads=["ws"], writes=["ws127"])
            S.emit("tensor", lambda e: e.matmul(ps_s01[:, 0:2], onesr[0:1, 0:128],
                                                wsA[0:1, 1:3], start=True, stop=True),
                   reads=["rws", "ws127", "onesr"], writes=["ps_s01_0"])
            S.emit("vector", lambda e: e.tensor_copy(out=bcast2[:], in_=ps_s01[:, 0:2]),
                   reads=["ps_s01_0"], writes=["bcast2"])

        def wquant_v(co):
            def ldw2(e, co=co):
                return e.dma_start(out=wstg[co % 2][:], in_=ww[128 * co:128 * (co + 1), :])
            S.emit("sync", ldw2, writes=[f"wstg{co % 2}"], sem=f"dw{co % 2}", dma=True)
            def uop(e, co=co):
                return e.tensor_scalar_mul(out=cbuf[:], in0=wstg[co % 2][:],
                                           scalar1=bcast2[:, 0:1])
            S.emit("vector", uop, reads=[f"wstg{co % 2}", "bcast2"], writes=["cbuf"])
            S.emit("vector", lambda e: e.tensor_scalar(
                out=sbuf_s[:], in0=cbuf[:], scalar1=0.5, scalar2=None, op0=OP.is_ge),
                reads=["cbuf"], writes=["s_a", "s_b"])
            S.emit("vector", lambda e: e.tensor_scalar(
                out=hb[0][:], in0=cbuf[:], scalar1=-0.5, scalar2=None, op0=OP.is_le),
                reads=["cbuf"], writes=["hb0_0", "hb0_1"])
            S.emit("vector", lambda e: e.tensor_tensor(
                out=aqT[:], in0=sbuf_s[:], in1=hb[0][:], op=OP.subtract),
                reads=["s_a", "s_b", "hb0_0", "hb0_1"], writes=["aqT"])

        def wquant_tp(co):
            parw = co % 2
            for ci in range(DC):
                def tpw(e, co=co, ci=ci, parw=parw):
                    return e.transpose(ps_t[parw][:, 128 * ci:128 * ci + 128],
                                       aqT[:, 128 * ci:128 * ci + 128], ident[:])
                S.emit("tensor", tpw, reads=["aqT", "ident"],
                       writes=[f"ps_t{parw}_s{ci}"])
            def evw(e, co=co, parw=parw):
                src = ps_t[parw][:].rearrange("p (ci f) -> p ci f", f=128)
                dst = wqT[:].rearrange("p (ci f) -> p ci f", f=1024)[:, :, 128 * co:128 * co + 128]
                return e.copy(out=dst, in_=src)
            S.emit("scalar", evw, reads=[f"ps_t{parw}_s{ci}" for ci in range(DC)],
                   writes=["wqT"])

        def wquant_chunk(co):
            wquant_v(co)
            wquant_tp(co)

        def p1_load(i):
            par = i % 2
            def ldx(e, i=i, par=par):
                return e.dma_start(out=xstg[par][:], in_=xb[128 * i:128 * (i + 1), :])
            S.emit("sync", ldx, writes=[f"xstg{par}"], sem=f"dx{par}", dma=True)

        def p1_stats(i):
            par = i % 2
            stp = st1[par]
            stok = [f"st1{par}_{k}" for k in range(18)]
            xt = f"xstg{par}"
            S.emit("vector", lambda e, par=par, stp=stp: e.bn_stats(
                out=stp[:, 0:6], in_=xstg[par][:, 0:512]),
                reads=[xt], writes=[stok[0]])
            S.emit("vector", lambda e, par=par, stp=stp: e.bn_stats(
                out=stp[:, 6:12], in_=xstg[par][:, 512:1024]),
                reads=[xt], writes=[stok[1]])
            S.emit("vector", lambda e, stp=stp: e.bn_aggr(
                out=stp[:, 12:14], in_=stp[:, 0:12]),
                reads=[stok[0], stok[1]], writes=[stok[12]])
            S.emit("vector", lambda e, stp=stp: e.tensor_scalar_mul(
                out=stp[:, 14:15], in0=stp[:, 12:13], scalar1=-1.0),
                reads=[stok[12]], writes=[stok[14]])
            S.emit("vector", lambda e, stp=stp: e.tensor_scalar_add(
                out=stp[:, 15:16], in0=stp[:, 13:14], scalar1=1e-5),
                reads=[stok[12]], writes=[stok[15]])
            S.emit("scalar", lambda e, stp=stp: e.activation(out=stp[:, 16:17], in_=stp[:, 15:16], func=AF.Ln),
                   reads=[stok[15]], writes=[stok[16]])
            S.emit("scalar", lambda e, stp=stp: e.activation(out=stp[:, 17:18], in_=stp[:, 16:17], func=AF.Exp, scale=-0.5),
                   reads=[stok[16]], writes=[stok[17]])

        def p1_xn(i):
            par = i % 2
            stp = st1[par]
            stok = [f"st1{par}_{k}" for k in range(18)]
            xt = f"xstg{par}"
            def xnop2(e, par=par, stp=stp):
                return e.tensor_scalar(out=aq[par][:], in0=xstg[par][:],
                                       scalar1=stp[:, 14:15], scalar2=stp[:, 17:18],
                                       op0=OP.add, op1=OP.mult)
            S.emit("vector", xnop2, reads=[xt, stok[14], stok[17]],
                   writes=[f"aq{par}", f"aq{par}_v", f"aq{par}_g"])
            for j in range(DC):
                def tpx(e, j=j, par=par):
                    return e.transpose(ps_t[par][:, 128 * j:128 * j + 128],
                                       aq[par][:, 128 * j:128 * j + 128], ident[:])
                S.emit("tensor", tpx, reads=[f"aq{par}", f"aq{par}_v", f"aq{par}_g", "ident"],
                       writes=[f"ps_t{par}_s{j}"])
            def evx(e, par=par):
                return e.copy(out=xnT[par][:], in_=ps_t[par][:])
            S.emit("scalar", evx, reads=[f"ps_t{par}_s{j}" for j in range(DC)],
                   writes=[f"xnT{par}"])

        def p1_mm(i):
            par = i % 2
            for half in range(2):
                def vmm(e, par=par, half=half):
                    last = None
                    for j in range(4 * half, 4 * half + 4):
                        for eh in range(2):
                            last = e.matmul(ps_a[eh],
                                            xnT[par][:, 128 * j:128 * j + 128],
                                            VOc[:, 1024 * j + 512 * eh:1024 * j + 512 * eh + 512],
                                            start=(j == 0), stop=(j == DC - 1))
                    return last
                S.emit("tensor", vmm,
                       reads=[f"xnT{par}"] + [f"VOc{j}" for j in range(4 * half, 4 * half + 4)],
                       writes=["ps_a0", "ps_a1"])
            for eh in range(2):
                def evv(e, i=i, eh=eh):
                    return e.copy(out=V[:, 1024 * i + 512 * eh:1024 * i + 512 * eh + 512],
                                  in_=ps_a[eh])
                S.emit("scalar", evv, reads=[f"ps_a{eh}"], writes=[f"V{i}e{eh}"])
            def kmm(e, par=par):
                last = None
                for j in range(DC):
                    last = e.matmul(ps_kq[0:64, 0:128],
                                    QKT[:, 512 + 64 * j:512 + 64 * j + 64],
                                    xnT[par][:, 128 * j:128 * j + 128],
                                    start=(j == 0), stop=(j == DC - 1))
                return last
            S.emit("tensor", kmm, reads=[f"xnT{par}", "QKT"], writes=["ps_kq_0"])
            def evk(e, i=i):
                return e.copy(out=xkT[0:64, 128 * i:128 * i + 128],
                              in_=ps_kq[0:64, 0:128])
            S.emit("scalar", evk, reads=["ps_kq_0"], writes=[f"xk{i}"])
            if i < nq:
                def qmm(e, par=par):
                    last = None
                    for j in range(DC):
                        last = e.matmul(ps_kq[0:64, 512:640],
                                        QKT[:, 64 * j:64 * j + 64],
                                        xnT[par][:, 128 * j:128 * j + 128],
                                        start=(j == 0), stop=(j == DC - 1))
                    return last
                S.emit("tensor", qmm, reads=[f"xnT{par}", "QKT"], writes=["ps_kq_1"])
                def evq(e, i=i):
                    return e.copy(out=xqT[0:64, 128 * i:128 * i + 128],
                                  in_=ps_kq[0:64, 512:640])
                S.emit("scalar", evq, reads=["ps_kq_1"], writes=[f"xq{i}"])

        p1_load(0)
        p1_stats(0)
        if nt > 1:
            p1_load(1)
        interleave = nt >= 28
        if not interleave:
            emit_vo_weights()
        for i in range(nt):
            p1_xn(i)
            if interleave and i >= 17 and i % 2 == 1 and (i - 17) // 2 < DC:
                wquant_v((i - 17) // 2)
            if i + 2 < nt:
                p1_load(i + 2)
            if i + 1 < nt:
                p1_stats(i + 1)
            if interleave and i == 0:
                for j in range(DC):
                    emit_vo_chunk(j)
            if i >= 1:
                p1_mm(i - 1)
            if interleave:
                if 8 <= i < 8 + DC:
                    wabs_chunk(i - 8)
                if i == 8 + DC:
                    wscale_finish()
                if i >= 17 and i % 2 == 1 and (i - 17) // 2 < DC:
                    wquant_tp((i - 17) // 2)
        p1_mm(nt - 1)
        if not interleave:
            for j in range(DC):
                wabs_chunk(j)
            wscale_finish()
            for co in range(DC):
                wquant_chunk(co)

        NKC = nt // 4
        ng = nt // 8

        def stA1a(t):
            par = t % 2
            def ldr(e, t=t, par=par):
                return e.dma_start(out=xres[par][:], in_=xb[128 * t:128 * (t + 1), :])
            S.emit("sync", ldr, writes=[f"xres{par}"], sem=f"dr{par}", dma=True)
            def smm0(e, t=t):
                return e.matmul(ps_s01[:, 0:512], xqT[0:64, 128 * t:128 * t + 128],
                                xkT[0:64, 0:512], start=True, stop=True)
            S.emit("tensor", smm0,
                   reads=[f"xq{t}"] + [f"xk{l}" for l in range(4)],
                   writes=["ps_s01_0"])

        def stA1ne(t):
            par = t % 2
            stp = st2[par]
            stok = [f"st2{par}_{k}" for k in range(28)]
            for hh in range(2):
                def expp(e, par=par, stp=stp, hh=hh):
                    return e.activation(
                        out=Pbf[:, 512 * hh:512 * hh + 512],
                        in_=ps_s01[:, 512 * hh:512 * hh + 512], func=AF.Exp,
                        bias=stp[:, 0:1],
                        accum_out=easum[:, 8 * par + 4 * hh:8 * par + 4 * hh + 1])
                S.emit("scalar", expp,
                       reads=[f"ps_s01_{hh}", stok[0]],
                       writes=[f"P{hh}", f"easum{par}_{4 * hh}"])

        def _a1b_chunk(t, c):
            par = t % 2
            stp = st2[par]
            stok = [f"st2{par}_{k}" for k in range(28)]
            def sbank(c):
                pa = c in (4, 5)
                tens = ps_a01 if pa else ps_s01
                off = 512 * (c % 2)
                return tens[:, off:off + 512], ("ps_a" if pa else "ps_s01_") + str(c % 2)
            if c > 0:
                def smm(e, t=t, c=c):
                    ap, _ = sbank(c)
                    return e.matmul(ap, xqT[0:64, 128 * t:128 * t + 128],
                                    xkT[0:64, 512 * c:512 * c + 512],
                                    start=True, stop=True)
                S.emit("tensor", smm,
                       reads=[f"xq{t}"] + [f"xk{4 * c + l}" for l in range(4)],
                       writes=[sbank(c)[1]])
            if c % 2 == 1 and c > 1:
                cp = c // 2
                pa = c in (4, 5)
                tens = ps_a01 if pa else ps_s01
                ttok = "ps_a" if pa else "ps_s01"
                def expp(e, cp=cp, par=par, stp=stp, tens=tens):
                    return e.activation(out=Pbf[:, 1024 * cp:1024 * cp + 1024],
                                        in_=tens[:], func=AF.Exp,
                                        bias=stp[:, 0:1],
                                        accum_out=easum[:, 8 * par + cp:8 * par + cp + 1])
                rd0 = ttok + ("0" if ttok == "ps_a" else "_0")
                rd1 = ttok + ("1" if ttok == "ps_a" else "_1")
                S.emit("scalar", expp,
                       reads=[rd0, rd1, stok[0]],
                       writes=[f"P{2 * cp}", f"P{2 * cp + 1}", f"easum{par}_{cp}"])

        def stA1b0(t):
            for c in range(min(2, NKC)):
                _a1b_chunk(t, c)

        def stA1b1(t):
            _tp_group(t, 0)
            for c in range(2, NKC):
                _a1b_chunk(t, c)

        def _tp_group(t, g):
            gp = g % 2
            for sl in range(8):
                kb = 8 * g + sl
                def tpp(e, kb=kb, sl=sl, gp=gp):
                    return e.transpose(ps_t[gp][:, 128 * sl:128 * sl + 128],
                                       Pbf[:, 128 * kb:128 * kb + 128], ident[:])
                S.emit("tensor", tpp, reads=[f"P{kb // 4}", "ident"],
                       writes=[f"ps_t{gp}_s{sl}"])
            def evp(e, g=g, gp=gp):
                return e.copy(out=PT[:, 1024 * g:1024 * (g + 1)], in_=ps_t[gp][:])
            S.emit("scalar", evp, reads=[f"ps_t{gp}_s{sl}" for sl in range(8)],
                   writes=[f"PT{g}"])

        def _pv_group(t, g, eh):
            def pvmm(e, g=g, eh=eh):
                last = None
                for sl in range(8):
                    kb = 8 * g + sl
                    last = e.matmul(
                        ps_a[eh], PT[:, 128 * kb:128 * kb + 128],
                        V[:, 1024 * kb + 512 * eh:1024 * kb + 512 * eh + 512],
                        start=(kb == 0), stop=(kb == nt - 1))
                return last
            S.emit("tensor", pvmm,
                   reads=[f"PT{g}"] + [f"V{8 * g + sl}e{eh}" for sl in range(8)],
                   writes=[f"ps_a{eh}"])

        def stB_tp(t):
            for g in range(1, ng):
                _tp_group(t, g)
                _pv_group(t, g - 1, 0)
            _pv_group(t, ng - 1, 0)

        def stB_pv(t, eh):
            if eh == 0:
                return
            for g in range(ng):
                _pv_group(t, g, eh)

        def stC1r(t):
            par = t % 2
            stp = st2[par]
            stok = [f"st2{par}_{k}" for k in range(28)]
            S.emit("vector", lambda e, stp=stp, par=par: e.reduce_sum(
                out=stp[:, 1:2], in_=easum[:, 8 * par:8 * par + 5], axis=AX),
                   reads=[f"easum{par}_{c}" for c in range(NKC // 2)]
                         + [f"easum{par}_4"], writes=[stok[1]])
            S.emit("vector", lambda e, stp=stp: e.reciprocal(out=stp[:, 2:3], in_=stp[:, 1:2]),
                   reads=[stok[1]], writes=[stok[2]])

        def stC1h(t, eh):
            par = t % 2
            stp = st2[par]
            stok = [f"st2{par}_{k}" for k in range(28)]
            def hop(e, eh=eh, par=par, stp=stp):
                return e.scalar_tensor_tensor(
                    out=hb[par][:, 512 * eh:512 * eh + 512], in0=ps_a[eh],
                    scalar=stp[:, 2:3], in1=xres[par][:, 512 * eh:512 * eh + 512],
                    op0=OP.mult, op1=OP.add)
            S.emit("vector", hop, reads=[f"ps_a{eh}", stok[2], f"xres{par}"],
                   writes=[f"hb{par}_{eh}"])
            def bno(e, eh=eh, par=par, stp=stp):
                return e.bn_stats(out=stp[:, 3 + 6 * eh:9 + 6 * eh],
                                  in_=hb[par][:, 512 * eh:512 * eh + 512])
            S.emit("vector", bno, reads=[f"hb{par}_{eh}"], writes=[stok[3 + eh]])

        def stC1a2(t):
            par = t % 2
            stp = st2[par]
            stok = [f"st2{par}_{k}" for k in range(28)]
            S.emit("vector", lambda e, stp=stp: e.bn_aggr(
                out=stp[:, 15:17], in_=stp[:, 3:15]),
                reads=[stok[3], stok[4]], writes=[stok[15]])
            S.emit("vector", lambda e, stp=stp: e.tensor_scalar_add(
                out=stp[:, 18:19], in0=stp[:, 16:17], scalar1=1e-5),
                reads=[stok[15]], writes=[stok[18]])
            S.emit("vector", lambda e, stp=stp: e.tensor_scalar_mul(
                out=stp[:, 17:18], in0=stp[:, 15:16], scalar1=-1.0),
                reads=[stok[15]], writes=[stok[17]])
            S.emit("scalar", lambda e, stp=stp: e.activation(out=stp[:, 19:20], in_=stp[:, 18:19], func=AF.Ln),
                   reads=[stok[18]], writes=[stok[19]])
            S.emit("scalar", lambda e, stp=stp: e.activation(out=stp[:, 20:21], in_=stp[:, 19:20], func=AF.Exp, scale=-0.5),
                   reads=[stok[19]], writes=[stok[20]])

        def stC1b_a(t):
            par = t % 2
            stp = st2[par]
            stok = [f"st2{par}_{k}" for k in range(28)]
            def uop2(e, par=par, stp=stp):
                return e.tensor_scalar_add(out=hn[par][:], in0=hb[par][:],
                                           scalar1=stp[:, 17:18])
            S.emit("vector", uop2, reads=[f"hb{par}_0", f"hb{par}_1", stok[17]],
                   writes=[f"hn{par}"])
            S.emit("vector", lambda e, par=par, stp=stp: e.reduce_max(
                out=stp[:, 21:22], in_=hn[par][:], axis=AX, apply_absolute_value=True),
                reads=[f"hn{par}"], writes=[stok[21]])

        def stC1b_b(t):
            par = t % 2
            stp = st2[par]
            stok = [f"st2{par}_{k}" for k in range(28)]
            S.emit("vector", lambda e, stp=stp: e.tensor_scalar(
                out=stp[:, 22:23], in0=stp[:, 21:22], scalar1=stp[:, 20:21],
                scalar2=1e-8, op0=OP.mult, op1=OP.add),
                reads=[stok[21], stok[20]], writes=[stok[22]])
            S.emit("vector", lambda e, stp=stp: e.reciprocal(out=stp[:, 26:27], in_=stp[:, 22:23]),
                   reads=[stok[22]], writes=[stok[26]])
            S.emit("vector", lambda e, stp=stp: e.tensor_scalar(
                out=stp[:, 23:24], in0=stp[:, 26:27], scalar1=stp[:, 20:21],
                scalar2=127.0, op0=OP.mult, op1=OP.mult),
                reads=[stok[26], stok[20]], writes=[stok[23]])
            def ypop(e, par=par, stp=stp):
                return e.tensor_scalar(out=yp[par][:], in0=hn[par][:],
                                       scalar1=stp[:, 23:24], scalar2=0.5,
                                       op0=OP.mult, op1=OP.add)
            S.emit("vector", ypop, reads=[f"hn{par}", stok[23]], writes=[f"yp{par}"])
            S.emit("vector", lambda e, stp=stp: e.tensor_tensor(
                out=stp[:, 24:25], in0=stp[:, 22:23], in1=bcast2[:, 1:2], op=OP.mult),
                reads=[stok[22], "bcast2"], writes=[stok[24]])

        def stC2(t):
            par = t % 2
            S.emit("vector", lambda e, par=par: e.tensor_scalar(
                out=cbuf[:], in0=yp[par][:], scalar1=0.0, scalar2=128.0,
                op0=OP.is_lt, op1=OP.mult),
                reads=[f"yp{par}"], writes=["cbuf"])
            S.emit("vector", lambda e, par=par: e.tensor_tensor(
                out=sbuf_s[:], in0=yp[par][:], in1=cbuf[:], op=OP.add),
                reads=[f"yp{par}", "cbuf"], writes=["s_a", "s_b"])
            for k in range(6, -1, -1):
                tw = float(1 << k)
                S.emit("vector", lambda e, tw=tw: e.tensor_scalar(
                    out=cbuf[:], in0=sbuf_s[:], scalar1=tw, scalar2=tw,
                    op0=OP.is_ge, op1=OP.mult),
                    reads=["s_a", "s_b"], writes=["cbuf"])
                S.emit("vector", lambda e: e.tensor_tensor(
                    out=sbuf_s[:], in0=sbuf_s[:], in1=cbuf[:], op=OP.subtract),
                    reads=["s_a", "s_b", "cbuf"], writes=["s_a", "s_b"])
            S.emit("vector", lambda e, par=par: e.scalar_tensor_tensor(
                out=aq[par][:], in0=sbuf_s[:], scalar=-1.0, in1=yp[par][:],
                op0=OP.mult, op1=OP.add),
                reads=["s_a", "s_b", f"yp{par}"],
                writes=[f"aq{par}", f"aq{par}_v", f"aq{par}_g"])

        def stD_tp(t):
            par = t % 2
            for j in range(DC):
                h, sl = divmod(j, 4)
                def tpa(e, j=j, h=h, sl=sl, par=par):
                    return e.transpose(ps_t[h][:, 128 * sl:128 * sl + 128],
                                       aq[par][:, 128 * j:128 * j + 128], ident[:])
                S.emit("tensor", tpa, reads=[f"aq{par}_v", f"aq{par}_g", "ident"],
                       writes=[f"ps_t{h}_s{sl}"])
            for h in range(2):
                def eva(e, h=h):
                    return e.copy(out=aqT[:, 512 * h:512 * h + 512],
                                  in_=ps_t[h][:, 0:512])
                S.emit("scalar", eva,
                       reads=[f"ps_t{h}_s{sl}" for sl in range(4)],
                       writes=[f"aqT_h{h}"])

        def stD_ffn(t):
            def fmm(e, j0, j1):
                last = None
                for j in range(j0, j1):
                    for oh in range(2):
                        last = e.matmul(
                            ps_kq[:, 512 * oh:512 * oh + 512],
                            aqT[:, 128 * j:128 * j + 128],
                            wqT[:, 1024 * j + 512 * oh:1024 * j + 512 * oh + 512],
                            start=(j == 0), stop=(j == DC - 1))
                return last
            S.emit("tensor", lambda e: fmm(e, 0, DC // 2),
                   reads=["aqT_h0", "wqT"], writes=["ps_kq_0", "ps_kq_1"])
            S.emit("tensor", lambda e: fmm(e, DC // 2, DC),
                   reads=["aqT_h1", "wqT"], writes=["ps_kq_0", "ps_kq_1"])

        def stD_out(t):
            par = t % 2
            stp = st2[par]
            stok = [f"st2{par}_{k}" for k in range(28)]
            for eh in range(2):
                def oop(e, eh=eh, par=par, stp=stp):
                    return e.scalar_tensor_tensor(
                        out=outb[par][:, 512 * eh:512 * eh + 512],
                        in0=ps_kq[:, 512 * eh:512 * eh + 512],
                        scalar=stp[:, 24:25],
                        in1=hb[par][:, 512 * eh:512 * eh + 512],
                        op0=OP.mult, op1=OP.add)
                S.emit("vector", oop, reads=[f"ps_kq_{eh}", stok[24],
                                             f"hb{par}_{eh}"],
                       writes=[f"outb{par}_e{eh}"])
            def stx(e, t=t, par=par):
                return e.dma_start(out=y[128 * t:128 * (t + 1), :], in_=outb[par][:])
            S.emit("gpsimd", stx, reads=[f"outb{par}_e0", f"outb{par}_e1"],
                   sem=f"do{par}", dma=True)

        stA1a(0)
        stA1b0(0)
        stA1ne(0)
        stA1b1(0)
        for t in range(nq):
            stB_tp(t)
            stC1r(t)
            stC1h(t, 0)
            stB_pv(t, 1)
            if t + 1 < nq:
                stA1a(t + 1)
                stA1b0(t + 1)
            stC1h(t, 1)
            if t > 0:
                stD_tp(t - 1)
            stC1a2(t)
            if t > 0:
                stD_ffn(t - 1)
            if t + 1 < nq:
                stA1ne(t + 1)
            stC1b_a(t)
            stC1b_b(t)
            if t > 0:
                stD_out(t - 1)
            if t + 1 < nq:
                stA1b1(t + 1)
            stC2(t)
        stD_tp(nq - 1)
        stD_ffn(nq - 1)
        stD_out(nq - 1)

        sems = {name: es.enter_context(nc.semaphore(f"sem_{name}"))
                for name in sorted(S.sems)}
        block = es.enter_context(nc.Block())
        final_waits = [(sn, S.count[sn]) for sn in sorted(S.count)
                       if sn.startswith("do")]

        def section(name):
            def body(eng):
                for waits, fn, sem, step in S.ops[name]:
                    for psem, pv in waits:
                        eng.wait_ge(sems[psem], pv)
                    inst = fn(eng)
                    inst.then_inc(sems[sem], step)
                if name == "gpsimd":
                    for sn, val in final_waits:
                        eng.wait_ge(sems[sn], val)
            return body

        block.sync(section("sync"))
        block.gpsimd(section("gpsimd"))
        block.vector(section("vector"))
        block.scalar(section("scalar"))
        block.tensor(section("tensor"))
    return nc


_NC_CACHE = {}


def get_nc(nt=32, nq=16):
    key = (nt, nq)
    if key not in _NC_CACHE:
        _NC_CACHE[key] = build_nc(nt, nq)
    return _NC_CACHE[key]


def kernel(L0_attn_x, L0_attn_Q, L0_attn_K, L0_attn_VO, L0_ffn_weight):
    x = np.ascontiguousarray(np.asarray(L0_attn_x, dtype=np.float32))
    q = np.ascontiguousarray(np.asarray(L0_attn_Q, dtype=np.float32))
    k = np.ascontiguousarray(np.asarray(L0_attn_K, dtype=np.float32))
    vo = np.ascontiguousarray(np.asarray(L0_attn_VO, dtype=np.float32))
    w = np.ascontiguousarray(np.asarray(L0_ffn_weight, dtype=np.float32))

    nc = get_nc()
    in_maps = []
    for c in range(8):
        b, h = c // 2, c % 2
        xb = np.ascontiguousarray(
            np.concatenate([x[b, 2048 * h:2048 * (h + 1)],
                            x[b, 2048 * (1 - h):2048 * (2 - h)]], axis=0))
        in_maps.append({"xb": xb, "qw": q, "kw": k, "vo": vo, "ww": w})
    res = run_bass_kernel_spmd(nc, in_maps, core_ids=list(range(8)))
    out = np.empty((4, 4096, 1024), dtype=np.float32)
    for c in range(8):
        b, h = c // 2, c % 2
        out[b, 2048 * h:2048 * (h + 1)] = res.results[c]["y"]
    return out
